# version 14
# baseline (speedup 1.0000x reference)
import numpy as np
import ml_dtypes
import concourse.bass as bass
import concourse.mybir as mybir
from concourse.alu_op_type import AluOpType as OP
from concourse.bass_utils import run_bass_kernel_spmd
from contextlib import ExitStack

F32 = mybir.dt.float32
BF16 = mybir.dt.bfloat16
I32 = mybir.dt.int32
AF = mybir.ActivationFunctionType
AX = mybir.AxisListType
BF = ml_dtypes.bfloat16

D = 1024
NH = 16
TOK = 2048
NT = 16
QT = 4
EPS = 1e-5
NEG = -1e30
MAGIC = 0x5F3759DF


class Sem:
    def __init__(self, ctx, nc, name):
        self.h = ctx.enter_context(nc.semaphore(name))
        self.n = 0

    def inc(self, inst, k=1):
        inst.then_inc(self.h, k)
        self.n += k
        return self.n


def build(debug=False, stage=99, a2sub=6):
    nc = bass.Bass("TRN2", target_bir_lowering=False, debug=False)

    dt = lambda name, shape, dty=F32: nc.dram_tensor(name, shape, dty, kind="ExternalInput").ap()
    x_d = dt("x", [TOK, D])
    wqT_d = dt("wqT", [D, D])
    wkT_d = dt("wkT", [D, D])
    wkp_d = dt("wkp", [D, D])
    wqp_d = dt("wqp", [D, D])
    wvT_d = dt("wvT", [D, D])
    woT_d = dt("woT", [D, D])
    w1p_d = dt("w1p", [D, 2 * D])
    winp_d = dt("winp", [D, 2 * D])
    wout_d = dt("wout", [D, D])
    wgate_d = dt("wgate", [D, 8])
    w2_d = dt("w2", [8 * D, D])
    bq_d = dt("bqc", [128, 8])
    bk_d = dt("bkc", [128, 8])
    bv_d = dt("bvr", [1, D])
    bo_d = dt("bor", [1, D])
    gb_d = dt("gbr", [1, 8])
    ident_d = nc.dram_tensor("ident", [128, 128], BF16, kind="ExternalInput").ap()
    mask_d = nc.dram_tensor("maskT", [128, 128], BF16, kind="ExternalInput").ap()
    ones_d = nc.dram_tensor("onesr", [1, 128], BF16, kind="ExternalInput").ap()
    sel_d = dt("sel", [128, 8])
    magic_d = nc.dram_tensor("magici", [128, 8], I32, kind="ExternalInput").ap()

    y_d = nc.dram_tensor("y", [TOK, D], F32, kind="ExternalOutput").ap()
    dbg = {}
    if debug:
        dbg["srun"] = nc.dram_tensor("dbg_srun", [128, 512], F32, kind="ExternalOutput").ap()
        dbg["sbase"] = nc.dram_tensor("dbg_sbase", [128, 512], F32, kind="ExternalOutput").ap()
        dbg["xattn"] = nc.dram_tensor("dbg_xattn", [TOK, D], F32, kind="ExternalOutput").ap()
        dbg["wdense"] = nc.dram_tensor("dbg_wdense", [128, 128], F32, kind="ExternalOutput").ap()
        dbg["hid"] = nc.dram_tensor("dbg_hid", [128, 8 * 1024], BF16, kind="ExternalOutput").ap()
        dbg["hsh"] = nc.dram_tensor("dbg_hsh", [128, 8 * 1024], BF16, kind="ExternalOutput").ap()
        dbg["accum"] = nc.dram_tensor("dbg_accum", [128, 8 * 1024], F32, kind="ExternalOutput").ap()
        dbg["xres"] = nc.dram_tensor("dbg_xres", [TOK, D], F32, kind="ExternalOutput").ap()

    cc_in = nc.dram_tensor("cc_in", [128, 512], F32)
    cc_out = nc.dram_tensor("cc_out", [256, 512], F32)

    with ExitStack() as g:
        eg = g.enter_context
        x_res = eg(nc.sbuf_tensor("x_res", [128, NT * D], F32))
        sapply = eg(nc.sbuf_tensor("sapply", [128, 2 * D], BF16))
        statsr = eg(nc.sbuf_tensor("statsr", [128, 192], F32))
        lnmv = eg(nc.sbuf_tensor("lnmv", [128, 32], F32))
        lnu = eg(nc.sbuf_tensor("lnu", [128, 16], F32))
        lnh = eg(nc.sbuf_tensor("lnh", [128, 16], F32))
        lnz = eg(nc.sbuf_tensor("lnz", [128, 16], F32))
        lnt = eg(nc.sbuf_tensor("lnt", [128, 16], F32))
        lnb = eg(nc.sbuf_tensor("lnb", [128, 16], F32))
        st2 = eg(nc.sbuf_tensor("st2", [128, 24], F32))
        smag = eg(nc.sbuf_tensor("smag", [128, 8], I32))
        ident = eg(nc.sbuf_tensor("ident_s", [128, 128], BF16))
        onesr = eg(nc.sbuf_tensor("onesr_s", [1, 128], BF16))
        bvr = eg(nc.sbuf_tensor("bvr_s", [1, D], BF16))
        bor = eg(nc.sbuf_tensor("bor_s", [1, D], BF16))
        gbr = eg(nc.sbuf_tensor("gbr_s", [1, 8], BF16))
        bqc = eg(nc.sbuf_tensor("bqc_s", [128, 8], F32))
        bkc = eg(nc.sbuf_tensor("bkc_s", [128, 8], F32))
        wgate_b = eg(nc.sbuf_tensor("wgate_b", [128, 64], BF16))
        selb = eg(nc.sbuf_tensor("sel_s", [128, 8], F32))

        sd_x = Sem(g, nc, "sd_x")
        sd_w = Sem(g, nc, "sd_w")
        s_wv = Sem(g, nc, "s_wv")
        s_row = Sem(g, nc, "s_row")
        s_ck0 = Sem(g, nc, "s_ck0")
        s_ck1 = Sem(g, nc, "s_ck1")
        s_wp0 = Sem(g, nc, "s_wp0")
        s_wp1 = Sem(g, nc, "s_wp1")
        s_wA = Sem(g, nc, "s_wA")
        s_wB = Sem(g, nc, "s_wB")
        sd_cc = Sem(g, nc, "sd_cc")
        sd_o = Sem(g, nc, "sd_o")
        st = Sem(g, nc, "st")
        sv = Sem(g, nc, "sv")
        sa = Sem(g, nc, "sa")
        sg = Sem(g, nc, "sg")
        sp = Sem(g, nc, "sp")

        LG, EB, PB, GE, MB = 0, 64, 128, 192, 256
        GMX, GSUM, GRS, GM1, GM2 = 320, 328, 336, 344, 352

        sap_war = [("t", 0), ("t", 0)]

        def wait_war(eng, war):
            kind, mark = war
            if mark:
                eng.wait_ge(st.h if kind == "t" else sv.h, mark)

        def emit_ln_chain(block, srcs, base, extra_wait_t=None, extra_wait_a=None):
            n = len(srcs)
            b = base

            def _(v):
                if extra_wait_t is not None:
                    v.wait_ge(st.h, extra_wait_t)
                if extra_wait_a is not None:
                    v.wait_ge(sa.h, extra_wait_a)
                v.wait_ge(sv.h, sv.n)

                def step(inst):
                    sv.inc(inst)
                    v.wait_ge(sv.h, sv.n)

                mm = None
                for i, (p1, p2) in enumerate(srcs):
                    v.bn_stats(statsr[:, (b + i) * 12:(b + i) * 12 + 6], p1)
                    mm = v.bn_stats(statsr[:, (b + i) * 12 + 6:(b + i) * 12 + 12], p2)
                step(mm)
                for i in range(n):
                    mm = v.bn_aggr(lnmv[:, (b + i) * 2:(b + i + 1) * 2],
                                   statsr[:, (b + i) * 12:(b + i + 1) * 12])
                step(mm)
                mv3 = lnmv[:].rearrange("p (i two) -> p two i", two=2)
                var_v = mv3[:, 1:2, b:b + n]
                mean_v = mv3[:, 0:1, b:b + n]
                step(v.tensor_scalar(lnu[:, b:b + n], var_v, EPS, None, OP.add))
                step(v.tensor_scalar(lnz[:, b:b + n].bitcast(I32),
                                     lnu[:, b:b + n].bitcast(I32), 1,
                                     None, OP.arith_shift_right))
                step(v.tensor_tensor(lnz[:, b:b + n].bitcast(I32), smag[:, 0:n],
                                     lnz[:, b:b + n].bitcast(I32), OP.subtract))
                step(v.tensor_scalar(lnh[:, b:b + n], lnu[:, b:b + n], 0.5, None, OP.mult))
                for _i in range(2):
                    step(v.tensor_tensor(lnt[:, b:b + n], lnz[:, b:b + n], lnz[:, b:b + n],
                                         OP.mult))
                    step(v.tensor_tensor(lnt[:, b:b + n], lnt[:, b:b + n], lnh[:, b:b + n],
                                         OP.mult))
                    step(v.tensor_scalar(lnt[:, b:b + n], lnt[:, b:b + n], -1.0, 1.5,
                                         OP.mult, OP.add))
                    step(v.tensor_tensor(lnz[:, b:b + n], lnz[:, b:b + n], lnt[:, b:b + n],
                                         OP.mult))
                step(v.scalar_tensor_tensor(lnb[:, b:b + n], mean_v, -1.0, lnz[:, b:b + n],
                                            OP.mult, OP.mult))
            block.vector(_)
            return sv.n

        def emit_apply(block, i, asrc, dst, war, m_chain):
            def _(v, i=i, asrc=asrc, dst=dst, war=war):
                wait_war(v, war)
                v.wait_ge(sv.h, sv.n)
                sv.inc(v.tensor_scalar(dst, asrc, lnz[:, i:i + 1], lnb[:, i:i + 1],
                                       OP.mult, OP.add))
            block.vector(_)
            return sv.n

        def emit_ln2(block, p1, p2, asrc, dst, war, extra_wait_t):
            def _(v):
                v.wait_ge(st.h, extra_wait_t)
                v.wait_ge(sv.h, sv.n)

                def step(inst):
                    sv.inc(inst)
                    v.wait_ge(sv.h, sv.n)

                v.bn_stats(st2[:, 0:6], p1)
                step(v.bn_stats(st2[:, 6:12], p2))
                step(v.bn_aggr(st2[:, 12:14], st2[:, 0:12]))
                step(v.tensor_scalar(st2[:, 14:15], st2[:, 13:14], EPS, None, OP.add))
                step(v.tensor_scalar(st2[:, 16:17].bitcast(I32), st2[:, 14:15].bitcast(I32),
                                     1, None, OP.arith_shift_right))
                step(v.tensor_tensor(st2[:, 16:17].bitcast(I32), smag[:, 0:1],
                                     st2[:, 16:17].bitcast(I32), OP.subtract))
                step(v.tensor_scalar(st2[:, 15:16], st2[:, 14:15], 0.5, None, OP.mult))
                for _i in range(2):
                    step(v.tensor_tensor(st2[:, 17:18], st2[:, 16:17], st2[:, 16:17], OP.mult))
                    step(v.tensor_tensor(st2[:, 17:18], st2[:, 17:18], st2[:, 15:16], OP.mult))
                    step(v.tensor_scalar(st2[:, 17:18], st2[:, 17:18], -1.0, 1.5,
                                         OP.mult, OP.add))
                    step(v.tensor_tensor(st2[:, 16:17], st2[:, 16:17], st2[:, 17:18], OP.mult))
                step(v.scalar_tensor_tensor(st2[:, 18:19], st2[:, 12:13], -1.0, st2[:, 16:17],
                                            OP.mult, OP.mult))
            block.vector(_)
            m_chain = sv.n

            def _(s, asrc=asrc, dst=dst, war=war, m_chain=m_chain):
                s.wait_ge(sv.h, m_chain)
                wait_war(s, war)
                sa.inc(s.activation(dst, asrc, AF.Identity,
                                    scale=st2[:, 16:17], bias=st2[:, 18:19]))
            block.scalar(_)
            return sa.n

        with ExitStack() as ascope:
            ea = ascope.enter_context
            kfm = ea(nc.sbuf_tensor("kfm", [128, 8 * TOK], BF16))
            ktm_buf = ea(nc.sbuf_tensor("ktm", [128, 1024], BF16))
            ktm_buf2 = ea(nc.sbuf_tensor("ktm2", [128, 1024], BF16))
            vtm = ea(nc.sbuf_tensor("vtm", [128, NT * D], BF16))
            snap = ea(nc.sbuf_tensor("snap", [128, NT * 512], BF16))
            srun = ea(nc.sbuf_tensor("srun", [128, 512], F32))
            sbase = ea(nc.sbuf_tensor("sbase", [128, 512], F32))
            maskT = ea(nc.sbuf_tensor("maskT_s", [128, 128], BF16))
            xlnf = ea(nc.sbuf_tensor("xlnf", [128, 8 * 512], BF16))
            qfm = ea(nc.sbuf_tensor("qfm", [128, 8 * 512], BF16))
            sst = ea(nc.sbuf_tensor("sst", [128, 1024], BF16))
            sattn = ea(nc.sbuf_tensor("sattn", [128, 1024], BF16))
            ybuf = ea(nc.sbuf_tensor("ybuf", [128, 2048], BF16))
            wbig = ea(nc.sbuf_tensor("wbig", [128, 8 * D], BF16))
            wdb = ea(nc.sbuf_tensor("wdb", [128, 2 * 1024], BF16))

            wv_mark = [0]
            wo_mark = [0]
            wchunk_marks = {}

            with nc.Block() as block:
                def _(sy):
                    sd_x.inc(sy.dma_start(
                        out=x_res[:].rearrange("p (t f) -> p t f", t=NT),
                        in_=x_d.rearrange("(t p) f -> p t f", p=128)), 16)
                    sd_x.inc(sy.dma_start(out=ident[:], in_=ident_d), 16)
                    sd_x.inc(sy.dma_start(out=maskT[:], in_=mask_d), 16)
                    sd_x.inc(sy.dma_start(out=onesr[:], in_=ones_d), 16)
                    sd_x.inc(sy.dma_start(out=bqc[:], in_=bq_d), 16)
                    sd_x.inc(sy.dma_start(out=bkc[:], in_=bk_d), 16)
                    sd_x.inc(sy.dma_start(out=selb[:], in_=sel_d), 16)
                    sd_x.inc(sy.dma_start(out=smag[:], in_=magic_d), 16)
                block.sync(_)
                x_mark = sd_x.n

                def _(gp):
                    wv_mark[0] = s_wv.inc(gp.dma_start(
                        out=wbig[:].rearrange("p (dc e) -> p dc e", dc=8),
                        in_=wvT_d.rearrange("(dc p) e -> p dc e", p=128)), 16)
                    s_row.inc(gp.dma_start(out=bvr[:], in_=bv_d), 16)
                    s_row.inc(gp.dma_start(out=bor[:], in_=bo_d), 16)
                    s_row.inc(gp.dma_start(out=gbr[:], in_=gb_d), 16)
                    s_row.inc(gp.dma_start(
                        out=wgate_b[:].rearrange("p (dc e) -> p dc e", dc=8),
                        in_=wgate_d.rearrange("(dc p) e -> p dc e", p=128)), 16)
                block.gpsimd(_)
                row_mark = s_row.n

                def _(v):
                    sv.inc(v.memset(srun[:], 0.0))
                block.vector(_)

            def emit_wchunk(gp, wsrc, buf_idx, key):
                ec = key[1]
                sck = s_ck0 if buf_idx == 0 else s_ck1
                wchunk_marks[key] = (sck, sck.inc(gp.dma_start(
                    out=wdb[:, buf_idx * 1024:(buf_idx + 1) * 1024],
                    in_=wsrc[ec * 128:(ec + 1) * 128, :]), 16))

            def emit_transpose(block, ptr, ptr_free, slot, dstview, apply_mark):
                def _(te, slot=slot, apply_mark=apply_mark, pf=ptr_free[0]):
                    te.wait_ge(sv.h, apply_mark)
                    te.wait_ge(sv.h, pf)
                    for cc in range(8):
                        mm = te.transpose(ptr[:, cc * 128:(cc + 1) * 128],
                                          sapply[:, slot * D + cc * 128:
                                                 slot * D + (cc + 1) * 128],
                                          ident[:])
                    st.inc(mm)
                block.tensor(_)
                m_tr = st.n

                def _(v, m_tr=m_tr, dstview=dstview):
                    v.wait_ge(st.h, m_tr)
                    sv.inc(v.tensor_copy(dstview,
                                         ptr[:].rearrange("p (c w) -> p c w", c=8)))
                block.vector(_)
                ptr_free[0] = sv.n
                return m_tr, sv.n

            if stage >= 1:
              with ExitStack() as ps:
                ptr = ps.enter_context(nc.psum_tensor("ptrA1", [128, 1024], BF16))
                pk0 = ps.enter_context(nc.psum_tensor("pk0", [128, 512], F32))
                pk1 = ps.enter_context(nc.psum_tensor("pk1", [128, 512], F32))
                pv0 = ps.enter_context(nc.psum_tensor("pv0", [128, 1024], F32))
                pv1 = ps.enter_context(nc.psum_tensor("pv1", [128, 1024], F32))
                pds = ps.enter_context(nc.psum_tensor("pds", [128, 512], F32))
                pks = [pk0, pk1]
                pvs = [pv0, pv1]

                with nc.Block() as block:
                    def _(gp):
                        emit_wchunk(gp, wkp_d, 0, ("k", 0, 0))
                        emit_wchunk(gp, wkp_d, 1, ("k", 1, 0))
                    block.gpsimd(_)

                    ptr_free = [sv.n]
                    pv_free = {0: sv.n, 1: sv.n}
                    pds_free = [sv.n]
                    kevac = {}
                    k_evac_all = []
                    vtm_marks = {}
                    ktm_evac = [0]
                    snap_cast = [0]
                    pend_ds = []
                    ktm_bufs = {0: ktm_buf[:, 0:1024], 1: ktm_buf2[:, 0:1024]}

                    def a_srcs(q):
                        return [(x_res[:, t * D:t * D + 512],
                                 x_res[:, t * D + 512:(t + 1) * D])
                                for t in range(q * QT, (q + 1) * QT)]

                    chain_marks = {}
                    def _(v):
                        v.wait_ge(sd_x.h, x_mark)
                    block.vector(_)
                    chain_marks[0] = emit_ln_chain(block, a_srcs(0), 0)

                    for q in range(4):
                        for trel in range(QT):
                            t = q * QT + trel
                            slot = trel % 2
                            m_ap = emit_apply(block, q * QT + trel,
                                              x_res[:, t * D:(t + 1) * D],
                                              sapply[:, slot * D:(slot + 1) * D],
                                              sap_war[slot], chain_marks[q])
                            dstv = xlnf[:].rearrange("p (c w) -> p c w", c=8)[
                                :, :, trel * 128:(trel + 1) * 128]
                            m_tr, _mv = emit_transpose(block, ptr, ptr_free, slot, dstv,
                                                       m_ap)
                            sap_war[slot] = ("t", m_tr)
                        xlnf_mark = sv.n
                        if q + 1 < 4:
                            chain_marks[q + 1] = emit_ln_chain(block, a_srcs(q + 1),
                                                               (q + 1) * QT)

                        for ec in range(8):
                            seq = q * 8 + ec
                            def _(te, ec=ec, q=q, seq=seq, xlnf_mark=xlnf_mark):
                                ck_s, ck_m = wchunk_marks[("k", ec, q)]
                                te.wait_ge(ck_s.h, ck_m)
                                te.wait_ge(sv.h, xlnf_mark)
                                if len(k_evac_all) >= 2:
                                    te.wait_ge(sv.h, k_evac_all[-2])
                                for cc in range(8):
                                    mm = te.matmul(pks[seq % 2][:, 0:512],
                                                   wdb[:, (seq % 2) * 1024 + cc * 128:
                                                       (seq % 2) * 1024 + (cc + 1) * 128],
                                                   xlnf[:, cc * 512:(cc + 1) * 512],
                                                   start=(cc == 0), stop=(cc == 7))
                                st.inc(mm)
                            block.tensor(_)
                            m_k = st.n

                            if seq + 2 < 32:
                                def _(gp, seq=seq, m_k=m_k):
                                    gp.wait_ge(st.h, m_k)
                                    q2, ec2 = divmod(seq + 2, 8)
                                    emit_wchunk(gp, wkp_d, (seq + 2) % 2, ("k", ec2, q2))
                                block.gpsimd(_)

                            def _(v, ec=ec, q=q, seq=seq, m_k=m_k):
                                v.wait_ge(st.h, m_k)
                                sv.inc(v.tensor_scalar(
                                    kfm[:, ec * TOK + q * 512:ec * TOK + (q + 1) * 512],
                                    pks[seq % 2][:, 0:512],
                                    bkc[:, ec:ec + 1], None, OP.add))
                            block.vector(_)
                            k_evac_all.append(sv.n)
                        kevac[q] = sv.n

                        for trel in range(QT):
                            t = q * QT + trel
                            pb = t % 2
                            def _(te, trel=trel, t=t, pb=pb, pvf=pv_free[pb]):
                                te.wait_ge(s_wv.h, wv_mark[0])
                                te.wait_ge(s_row.h, row_mark)
                                te.wait_ge(sv.h, pvf)
                                for j in range(2):
                                    for cc in range(8):
                                        te.matmul(pvs[pb][:, j * 512:(j + 1) * 512],
                                                  xlnf[:, cc * 512 + trel * 128:
                                                       cc * 512 + (trel + 1) * 128],
                                                  wbig[:, cc * D + j * 512:cc * D + (j + 1) * 512],
                                                  start=(cc == 0), stop=False)
                                    mm = te.matmul(pvs[pb][:, j * 512:(j + 1) * 512],
                                                   onesr[:],
                                                   bvr[0:1, j * 512:(j + 1) * 512],
                                                   start=False, stop=True)
                                st.inc(mm)
                            block.tensor(_)
                            m_vmm = st.n

                            def _(v, t=t, pb=pb, m_vmm=m_vmm):
                                v.wait_ge(st.h, m_vmm)
                                sv.inc(v.tensor_copy(vtm[:, t * D:(t + 1) * D],
                                                     pvs[pb][:, 0:1024]))
                            block.vector(_)
                            pv_free[pb] = sv.n
                            vtm_marks[t] = sv.n

                            def _(te, t=t, q=q, pf=ptr_free[0]):
                                te.wait_ge(sv.h, kevac[q])
                                te.wait_ge(sv.h, pf)
                                for ec in range(8):
                                    mm = te.transpose(
                                        ptr[:, ec * 128:(ec + 1) * 128],
                                        kfm[:, ec * TOK + t * 128:ec * TOK + (t + 1) * 128],
                                        ident[:])
                                st.inc(mm)
                            block.tensor(_)
                            m_ktr = st.n

                            def _(v, t=t, m_ktr=m_ktr):
                                v.wait_ge(st.h, m_ktr)
                                sv.inc(v.tensor_copy(ktm_bufs[t % 2], ptr[:, 0:1024]))
                            block.vector(_)
                            ptr_free[0] = sv.n
                            ktm_evac[0] = sv.n

                            def emit_ds(tds, ke, kb):
                                def _(te, tds=tds, ke=ke, kb=kb, pdf=pds_free[0]):
                                    te.wait_ge(sv.h, ke)
                                    te.wait_ge(sv.h, vtm_marks[tds])
                                    te.wait_ge(sv.h, pdf)
                                    for h in range(NH):
                                        rb = (h % 2) * 64
                                        cb = (h // 2) * 64
                                        mm = te.matmul(
                                            pds[rb:rb + 64, cb:cb + 64],
                                            kb[:, (h // 2) * 128 + (h % 2) * 64:
                                               (h // 2) * 128 + (h % 2) * 64 + 64],
                                            vtm[:, tds * D + h * 64:tds * D + (h + 1) * 64],
                                            start=True, stop=True)
                                    st.inc(mm)
                                block.tensor(_)
                                m_ds = st.n

                                def _(v, tds=tds, m_ds=m_ds):
                                    v.wait_ge(st.h, m_ds)
                                    v.wait_ge(sv.h, sv.n)
                                    sv.inc(v.tensor_tensor(srun[:], pds[0:128, 0:512],
                                                           srun[:], OP.add))
                                    v.wait_ge(sv.h, sv.n)
                                    sv.inc(v.tensor_copy(snap[:, tds * 512:(tds + 1) * 512],
                                                         srun[:]))
                                block.vector(_)
                                pds_free[0] = sv.n

                            pend_ds.append((t, ktm_evac[0], ktm_bufs[t % 2]))
                            if len(pend_ds) > 1:
                                emit_ds(*pend_ds.pop(0))

                    while pend_ds:
                        emit_ds(*pend_ds.pop(0))

            def emit_cc_gp(block):
                a1_a = sa.n
                a1_v = sv.n

                def _(gp):
                    gp.wait_ge(sa.h, a1_a)
                    gp.wait_ge(sv.h, a1_v)
                    m = sd_cc.inc(gp.dma_start(out=cc_in.ap(), in_=srun[:]), 16)
                    if stage >= 3:
                        emit_wchunk(gp, wqp_d, 0, ("q", 0, 0))
                        emit_wchunk(gp, wqp_d, 1, ("q", 1, 0))
                    gp.wait_ge(sd_cc.h, m)
                    sg.inc(gp.collective_compute(
                        "AllGather", OP.bypass,
                        replica_groups=[[0, 1], [2, 3], [4, 5], [6, 7]],
                        ins=[cc_in.ap().opt()],
                        outs=[cc_out.ap().opt()]))
                    gp.wait_ge(sg.h, sg.n)
                    wo_mark[0] = s_wv.inc(gp.dma_start(
                        out=wbig[:].rearrange("p (dc e) -> p dc e", dc=8),
                        in_=woT_d.rearrange("(dc p) e -> p dc e", p=128)), 16)
                    m2 = sd_cc.inc(gp.dma_start(out=sbase[:], in_=cc_out.ap()[0:128, :]), 16)
                    gp.wait_ge(sd_cc.h, m2)
                block.gpsimd(_)

                def _(v):
                    v.wait_ge(sd_cc.h, sd_cc.n)
                    v.wait_ge(sd_x.h, x_mark)
                    sv.inc(v.tensor_scalar(sbase[:], sbase[:], selb[:, 0:1], None, OP.mult))
                    v.wait_ge(sv.h, sv.n)
                block.vector(_)
                return sv.n

            def emit_fold(block, pool_mark):
                def _(v, pool_mark=pool_mark):
                    v.wait_ge(sv.h, pool_mark)
                    for t in range(NT - 1, 0, -1):
                        sv.inc(v.tensor_tensor(snap[:, t * 512:(t + 1) * 512],
                                               sbase[:],
                                               snap[:, (t - 1) * 512:t * 512], OP.add))
                        v.wait_ge(sv.h, sv.n)
                    sv.inc(v.tensor_copy(snap[:, 0:512], sbase[:]))
                block.vector(_)
                return sv.n

            fold_mark_g = [0]
            if stage >= 2:
                with nc.Block() as block:
                    pm = emit_cc_gp(block)
                    fold_mark_g[0] = emit_fold(block, pm)
                    if debug:
                        def _(sy):
                            sy.wait_ge(sv.h, sv.n)
                            sd_o.inc(sy.dma_start(out=dbg["srun"], in_=srun[:]), 16)
                            sd_o.inc(sy.dma_start(out=dbg["sbase"], in_=sbase[:]), 16)
                        block.sync(_)

            if stage >= 3:
              with ExitStack() as ps:
                ptr = ps.enter_context(nc.psum_tensor("ptrA2", [128, 1024], BF16))
                pq0 = ps.enter_context(nc.psum_tensor("pq0", [128, 512], F32))
                pq1 = ps.enter_context(nc.psum_tensor("pq1", [128, 512], F32))
                psc = ps.enter_context(nc.psum_tensor("psc", [128, 1024], F32))
                pattn = ps.enter_context(nc.psum_tensor("pattn", [128, 512], F32))
                py = ps.enter_context(nc.psum_tensor("py", [128, 1024], F32))
                pqs = [pq0, pq1]

                with nc.Block() as block:
                    fold_mark = [fold_mark_g[0]]

                    ptr_free = [sv.n]
                    py_free = [("v", 0)]
                    ybuf_war = [0]
                    ln2war = {0: 0, 1: 0}
                    q_evac_all = []
                    psc_free = [sv.n]
                    pattn_free = [sv.n]

                    def a_srcs(q):
                        return [(x_res[:, t * D:t * D + 512],
                                 x_res[:, t * D + 512:(t + 1) * D])
                                for t in range(q * QT, (q + 1) * QT)]

                    chain_marks = {}
                    chain_marks[0] = emit_ln_chain(block, a_srcs(0), 0)
                    pend_wo = [None]

                    def emit_wo_stage(t, trel, am):
                        def _(te, m=am, pyf=py_free[0]):
                            te.wait_ge(sv.h, m)
                            te.wait_ge(s_wv.h, wo_mark[0])
                            te.wait_ge(s_row.h, row_mark)
                            wait_war(te, pyf)
                            for j in range(2):
                                for cc in range(8):
                                    te.matmul(py[:, j * 512:(j + 1) * 512],
                                              sattn[:, cc * 128:(cc + 1) * 128],
                                              wbig[:, cc * D + j * 512:cc * D + (j + 1) * 512],
                                              start=(cc == 0), stop=False)
                                mm = te.matmul(py[:, j * 512:(j + 1) * 512],
                                               onesr[:],
                                               bor[0:1, j * 512:(j + 1) * 512],
                                               start=False, stop=True)
                            st.inc(mm)
                        block.tensor(_)
                        m_wo = st.n

                        if a2sub < 6:
                            return
                        sy_ = trel % 2

                        def _(v, m_wo=m_wo, sy_=sy_):
                            v.wait_ge(st.h, m_wo)
                            sv.inc(v.tensor_copy(
                                ybuf[:, sy_ * 1024:(sy_ + 1) * 1024], py[:, 0:1024]))
                        block.vector(_)
                        py_free[0] = ("v", sv.n)

                        if trel % 2 == 1:
                            par = (trel // 2) % 2
                            base2 = par * 2
                            chP = emit_ln_chain(
                                block,
                                [(ybuf[:, 0:512], ybuf[:, 512:1024]),
                                 (ybuf[:, 1024:1536], ybuf[:, 1536:2048])],
                                base2)
                            aps = []
                            res_m = 0
                            for j in range(2):
                                tt = t - 1 + j
                                m_apj = emit_apply(block, base2 + j,
                                                   ybuf[:, j * 1024:(j + 1) * 1024],
                                                   ybuf[:, j * 1024:(j + 1) * 1024],
                                                   ("t", 0), chP)
                                aps.append(m_apj)

                                def _(v, tt=tt, j=j):
                                    v.wait_ge(sv.h, sv.n)
                                    sv.inc(v.tensor_tensor(
                                        x_res[:, tt * D:(tt + 1) * D],
                                        x_res[:, tt * D:(tt + 1) * D],
                                        ybuf[:, j * 1024:(j + 1) * 1024], OP.add))
                                block.vector(_)
                                res_m = sv.n
                            ln2war[par] = aps[-1]
                            ybuf_war[0] = aps[-1]
                            ybuf_res = res_m

                    for q in range(4):
                        for trel in range(QT):
                            t = q * QT + trel
                            slot = trel % 2
                            m_ap = emit_apply(block, q * QT + trel,
                                              x_res[:, t * D:(t + 1) * D],
                                              sapply[:, slot * D:(slot + 1) * D],
                                              sap_war[slot], chain_marks[q])
                            dstv = xlnf[:].rearrange("p (c w) -> p c w", c=8)[
                                :, :, trel * 128:(trel + 1) * 128]
                            m_tr, _mv = emit_transpose(block, ptr, ptr_free, slot, dstv,
                                                       m_ap)
                            sap_war[slot] = ("t", m_tr)
                        xlnf_mark = sv.n
                        if q + 1 < 4:
                            chain_marks[q + 1] = emit_ln_chain(block, a_srcs(q + 1),
                                                               (q + 1) * QT)

                        for ec in range(8 if a2sub >= 2 else 0):
                            seq = q * 8 + ec
                            def _(te, ec=ec, q=q, seq=seq, xlnf_mark=xlnf_mark):
                                ck_s, ck_m = wchunk_marks[("q", ec, q)]
                                te.wait_ge(ck_s.h, ck_m)
                                te.wait_ge(sv.h, xlnf_mark)
                                if len(q_evac_all) >= 2:
                                    te.wait_ge(sv.h, q_evac_all[-2])
                                for cc in range(8):
                                    mm = te.matmul(pqs[seq % 2][:, 0:512],
                                                   wdb[:, (seq % 2) * 1024 + cc * 128:
                                                       (seq % 2) * 1024 + (cc + 1) * 128],
                                                   xlnf[:, cc * 512:(cc + 1) * 512],
                                                   start=(cc == 0), stop=(cc == 7))
                                st.inc(mm)
                            block.tensor(_)
                            m_q = st.n

                            if seq + 2 < 32:
                                def _(gp, seq=seq, m_q=m_q):
                                    gp.wait_ge(st.h, m_q)
                                    q2, ec2 = divmod(seq + 2, 8)
                                    emit_wchunk(gp, wqp_d, (seq + 2) % 2, ("q", ec2, q2))
                                block.gpsimd(_)

                            def _(v, ec=ec, seq=seq, m_q=m_q):
                                v.wait_ge(st.h, m_q)
                                sv.inc(v.tensor_scalar(qfm[:, ec * 512:(ec + 1) * 512],
                                                       pqs[seq % 2][:, 0:512],
                                                       bqc[:, ec:ec + 1], None, OP.add))
                            block.vector(_)
                            q_evac_all.append(sv.n)
                        qdone = sv.n

                        if pend_wo[0] is not None:
                            emit_wo_stage(*pend_wo[0])
                            pend_wo[0] = None

                        for trel in range(QT if a2sub >= 3 else 0):
                            t = q * QT + trel
                            attn_marks = []
                            sstb = [sst, ktm_buf]
                            sst_marks = {}
                            for hg in range(2):
                                def _(te, t=t, trel=trel, hg=hg, qdone=qdone,
                                      pf=psc_free[0]):
                                    te.wait_ge(sv.h, qdone)
                                    te.wait_ge(sv.h, pf)
                                    for hrel in range(8):
                                        h = hg * 8 + hrel
                                        rb = (h % 2) * 64
                                        ec = h // 2
                                        pos = (hrel % 2) * 512 + (hrel // 2) * 128
                                        mm = te.matmul(
                                            psc[:, pos:pos + 128],
                                            kfm[rb:rb + 64, ec * TOK + t * 128:
                                                ec * TOK + (t + 1) * 128],
                                            qfm[rb:rb + 64, ec * 512 + trel * 128:
                                                ec * 512 + (trel + 1) * 128],
                                            start=True, stop=True)
                                    st.inc(mm)
                                block.tensor(_)
                                m_sc = st.n

                                def _(v, hg=hg, m_sc=m_sc):
                                    v.wait_ge(st.h, m_sc)
                                    for hrel in range(8):
                                        pos = (hrel % 2) * 512 + (hrel // 2) * 128
                                        mm = v.tensor_tensor(
                                            sstb[hg][:, pos:pos + 128],
                                            psc[:, pos:pos + 128],
                                            maskT[:], OP.mult)
                                    sv.inc(mm)
                                block.vector(_)
                                sst_marks[hg] = sv.n
                                psc_free[0] = sv.n

                                if hg == 0 and pend_wo[0] is not None:
                                    emit_wo_stage(*pend_wo[0])
                                    pend_wo[0] = None

                            if a2sub < 4:
                                continue

                            for hg in range(2):
                                def _(te, t=t, trel=trel, hg=hg, m_sstv=sst_marks[hg],
                                      paf=pattn_free[0], fm=fold_mark[0]):
                                    te.wait_ge(sv.h, m_sstv)
                                    te.wait_ge(sv.h, paf)
                                    te.wait_ge(sv.h, fm)
                                    for hrel in range(8):
                                        h = hg * 8 + hrel
                                        rb = (h % 2) * 64
                                        ec = h // 2
                                        ecrel = ec - hg * 4
                                        pos = (hrel % 2) * 512 + (hrel // 2) * 128
                                        te.matmul(
                                            pattn[rb:rb + 64, ecrel * 128:(ecrel + 1) * 128],
                                            vtm[:, t * D + h * 64:t * D + (h + 1) * 64],
                                            sstb[hg][:, pos:pos + 128],
                                            start=True, stop=False)
                                        mm = te.matmul(
                                            pattn[rb:rb + 64, ecrel * 128:(ecrel + 1) * 128],
                                            snap[rb:rb + 64, t * 512 + ec * 64:
                                                 t * 512 + ec * 64 + 64],
                                            qfm[rb:rb + 64, ec * 512 + trel * 128:
                                                ec * 512 + (trel + 1) * 128],
                                            start=False, stop=True)
                                    st.inc(mm)
                                block.tensor(_)
                                m_at = st.n

                                def _(v, hg=hg, m_at=m_at):
                                    v.wait_ge(st.h, m_at)
                                    sv.inc(v.tensor_copy(sattn[:, hg * 512:(hg + 1) * 512],
                                                         pattn[:, 0:512]))
                                block.vector(_)
                                attn_marks.append(sv.n)
                                pattn_free[0] = sv.n

                            if a2sub < 5:
                                continue
                            pend_wo[0] = (t, trel, attn_marks[-1])

                    if pend_wo[0] is not None:
                        emit_wo_stage(*pend_wo[0])
                        pend_wo[0] = None

            if debug and stage >= 3:
                with nc.Block() as block:
                    def _(sy):
                        sy.wait_ge(sv.h, sv.n)
                        sd_o.inc(sy.dma_start(
                            out=dbg["xattn"].rearrange("(t p) f -> p t f", p=128),
                            in_=x_res[:].rearrange("p (t f) -> p t f", t=NT)), 16)
                    block.sync(_)

        if stage >= 4:
          with ExitStack() as fscope:
            ef = fscope.enter_context
            xlnfF = ef(nc.sbuf_tensor("xlnfF", [128, 8 * 1024], BF16))
            gwork = ef(nc.sbuf_tensor("gwork", [128, 384], F32))
            hid = ef(nc.sbuf_tensor("hid", [128, 8 * 1024], BF16))
            hsh = ef(nc.sbuf_tensor("hsh", [128, 8 * 1024], BF16))
            accum = ef(nc.sbuf_tensor("accum", [128, 8 * 1024], F32))
            wdense = ef(nc.sbuf_tensor("wdense", [128, 128], F32))
            wpair = ef(nc.sbuf_tensor("wpair", [128, 2 * 2048], BF16))
            wout_s = ef(nc.sbuf_tensor("wout_s", [128, 8 * 1024], BF16))
            w2A = ef(nc.sbuf_tensor("w2A", [128, 8 * 1024], BF16))
            ssil = ef(nc.sbuf_tensor("ssil", [128, 1024], F32))

            fmarks = {}
            hid_mark = 0
            f1_state = {}

            def make_f1ab_steps(hf, base0):
                T0 = hf * 8
                WD = hf * 64
                steps = []

                def f_srcs(hg):
                    return [(x_res[:, t * D:t * D + 512],
                             x_res[:, t * D + 512:(t + 1) * D])
                            for t in range(T0 + hg * 4, T0 + (hg + 1) * 4)]

                for half_grp in range(2):
                    def chain_step(block, ptr, pg, hg=half_grp):
                        f1_state[(hf, "chain", hg)] = emit_ln_chain(
                            block, f_srcs(hg), base0 + hg * 4)
                    steps.append(chain_step)

                    for k in range(QT):
                        def apply_step(block, ptr, pg, hg=half_grp, k=k):
                            trel = hg * 4 + k
                            t = T0 + trel
                            slot = trel % 2
                            m_ap = emit_apply(block, base0 + trel,
                                              x_res[:, t * D:(t + 1) * D],
                                              sapply[:, slot * D:(slot + 1) * D],
                                              sap_war[slot], f1_state[(hf, "chain", hg)])
                            f1_state[(hf, "ap", trel)] = m_ap
                            f1_state[(hf, "apslot", trel)] = slot
                        steps.append(apply_step)

                        def tr_step(block, ptr, pg, hg=half_grp, k=k):
                            trel = hg * 4 + k
                            slot = f1_state[(hf, "apslot", trel)]
                            pf = f1_state.setdefault((hf, "ptr_free"), [sv.n])
                            dstv = xlnfF[:].rearrange("p (c w) -> p c w", c=8)[
                                :, :, trel * 128:(trel + 1) * 128]
                            m_tr, m_xl = emit_transpose(block, ptr, pf, slot, dstv,
                                                        f1_state[(hf, "ap", trel)])
                            sap_war[slot] = ("t", m_tr)
                            f1_state[(hf, "xl", trel)] = m_xl
                        steps.append(tr_step)

                        def gate_step(block, ptr, pg, hg=half_grp, k=k):
                            trel = hg * 4 + k
                            pgf = f1_state.setdefault((hf, "pg_free"), [sv.n])

                            def _(te, trel=trel, m_xl=f1_state[(hf, "xl", trel)],
                                  pgfv=pgf[0]):
                                te.wait_ge(sv.h, m_xl)
                                te.wait_ge(sv.h, pgfv)
                                te.wait_ge(s_row.h, row_mark)
                                for cc in range(8):
                                    te.matmul(pg[:, 0:8],
                                              xlnfF[:, cc * 1024 + trel * 128:
                                                   cc * 1024 + (trel + 1) * 128],
                                              wgate_b[:, cc * 8:(cc + 1) * 8],
                                              start=(cc == 0), stop=False)
                                mm = te.matmul(pg[:, 0:8], onesr[:], gbr[0:1, 0:8],
                                               start=False, stop=True)
                                st.inc(mm)
                            block.tensor(_)
                            m_g = st.n

                            def _(v, trel=trel, m_g=m_g):
                                v.wait_ge(st.h, m_g)
                                v.wait_ge(sv.h, sv.n)
                                sv.inc(v.tensor_copy(
                                    gwork[:, LG + trel * 8:LG + (trel + 1) * 8],
                                    pg[:, 0:8]))
                            block.vector(_)
                            pgf[0] = sv.n
                        steps.append(gate_step)

                def softmax_step(block, ptr, pg):
                    def _(v):
                        def step(inst):
                            sv.inc(inst)
                            v.wait_ge(sv.h, sv.n)
                        v.wait_ge(sv.h, sv.n)
                        mm = None
                        for ti in range(8):
                            mm = v.reduce_max(gwork[:, GMX + ti:GMX + ti + 1],
                                              gwork[:, LG + ti * 8:LG + (ti + 1) * 8],
                                              axis=AX.X)
                        step(mm)
                        for ti in range(8):
                            mm = v.tensor_scalar(gwork[:, EB + ti * 8:EB + (ti + 1) * 8],
                                                 gwork[:, LG + ti * 8:LG + (ti + 1) * 8],
                                                 gwork[:, GMX + ti:GMX + ti + 1], None,
                                                 OP.subtract)
                        step(mm)
                    block.vector(_)
                    m_pre = sv.n

                    def _(s, m_pre=m_pre):
                        s.wait_ge(sv.h, m_pre)
                        sa.inc(s.activation(gwork[:, PB:PB + 64], gwork[:, EB:EB + 64],
                                            AF.Exp))
                    block.scalar(_)
                    m_exp = sa.n

                    def _(v, m_exp=m_exp):
                        def step(inst):
                            sv.inc(inst)
                            v.wait_ge(sv.h, sv.n)
                        v.wait_ge(sa.h, m_exp)
                        v.wait_ge(sv.h, sv.n)
                        mm = None
                        for ti in range(8):
                            mm = v.reduce_sum(gwork[:, GSUM + ti:GSUM + ti + 1],
                                              gwork[:, PB + ti * 8:PB + (ti + 1) * 8],
                                              axis=AX.X)
                        step(mm)
                        step(v.reciprocal(gwork[:, GRS:GRS + 8], gwork[:, GSUM:GSUM + 8]))
                        for ti in range(8):
                            mm = v.tensor_scalar(gwork[:, PB + ti * 8:PB + (ti + 1) * 8],
                                                 gwork[:, PB + ti * 8:PB + (ti + 1) * 8],
                                                 gwork[:, GRS + ti:GRS + ti + 1], None,
                                                 OP.mult)
                        step(mm)
                        for ti in range(8):
                            mm = v.reduce_max(gwork[:, GM1 + ti:GM1 + ti + 1],
                                              gwork[:, PB + ti * 8:PB + (ti + 1) * 8],
                                              axis=AX.X)
                        step(mm)
                        for ti in range(8):
                            mm = v.tensor_scalar(gwork[:, GE + ti * 8:GE + (ti + 1) * 8],
                                                 gwork[:, PB + ti * 8:PB + (ti + 1) * 8],
                                                 gwork[:, GM1 + ti:GM1 + ti + 1], None,
                                                 OP.is_ge)
                        step(mm)
                        for ti in range(8):
                            mm = v.scalar_tensor_tensor(
                                gwork[:, MB + ti * 8:MB + (ti + 1) * 8],
                                gwork[:, GE + ti * 8:GE + (ti + 1) * 8], NEG,
                                gwork[:, PB + ti * 8:PB + (ti + 1) * 8],
                                OP.mult, OP.add)
                        step(mm)
                        for ti in range(8):
                            mm = v.reduce_max(gwork[:, GM2 + ti:GM2 + ti + 1],
                                              gwork[:, MB + ti * 8:MB + (ti + 1) * 8],
                                              axis=AX.X)
                        step(mm)
                        for ti in range(8):
                            mm = v.tensor_scalar(gwork[:, GE + ti * 8:GE + (ti + 1) * 8],
                                                 gwork[:, PB + ti * 8:PB + (ti + 1) * 8],
                                                 gwork[:, GM2 + ti:GM2 + ti + 1], None,
                                                 OP.is_ge)
                        step(mm)
                        for ti in range(8):
                            mm = v.tensor_tensor(wdense[:, WD + ti * 8:WD + (ti + 1) * 8],
                                                 gwork[:, PB + ti * 8:PB + (ti + 1) * 8],
                                                 gwork[:, GE + ti * 8:GE + (ti + 1) * 8],
                                                 OP.mult)
                        step(mm)
                    block.vector(_)
                steps.append(softmax_step)
                return steps

            f1ab_steps = {0: make_f1ab_steps(0, 0), 1: make_f1ab_steps(1, 8)}

            for hf in range(2):
                if hf == 1 and stage < 6:
                    continue
                T0 = hf * 8
                with ExitStack() as ps:
                    ptr = ps.enter_context(nc.psum_tensor(f"ptrF{hf}", [128, 1024], BF16))
                    pg = ps.enter_context(nc.psum_tensor(f"pg{hf}", [128, 8], F32))
                    pua0 = ps.enter_context(nc.psum_tensor(f"pua0{hf}", [128, 512], F32))
                    pub0 = ps.enter_context(nc.psum_tensor(f"pub0{hf}", [128, 512], F32))
                    pua1 = ps.enter_context(nc.psum_tensor(f"pua1{hf}", [128, 512], F32))
                    pub1 = ps.enter_context(nc.psum_tensor(f"pub1{hf}", [128, 512], F32))
                    puas = [pua0, pua1]
                    pubs = [pub0, pub1]
                    psh = ps.enter_context(nc.psum_tensor(f"psh{hf}", [128, 1024], F32))

                    def emit_pair(gp, wkey, hf, pp, src_d):
                        swp = s_wp0 if pp % 2 == 0 else s_wp1
                        fmarks[(wkey, hf, pp)] = (swp, swp.inc(gp.dma_start(
                            out=wpair[:, (pp % 2) * 2048:(pp % 2) * 2048 + 2048]
                                .rearrange("p (dc e) -> p dc e", dc=8),
                            in_=src_d[:, pp * 256:(pp + 1) * 256]
                                .rearrange("(dc p) e -> p dc e", p=128)), 16))

                    with nc.Block() as block:
                        def _(gp, hf=hf):
                            emit_pair(gp, "w1", hf, 0, w1p_d)
                            emit_pair(gp, "w1", hf, 1, w1p_d)
                        block.gpsimd(_)

                        for stp in f1ab_steps[hf]:
                            stp(block, ptr, pg)

                        pua_free = {0: sa.n, 1: sa.n}
                        pub_free = {0: sv.n, 1: sv.n}
                        ub_ix = [0]
                        for dst, wkey, wsrc_d in ((hid, "w1", w1p_d), (hsh, "win", winp_d)):
                            if wkey == "win":
                                def _(gp, hf=hf, m=fmarks[("w1mm", hf)]):
                                    gp.wait_ge(st.h, m)
                                    emit_pair(gp, "win", hf, 0, winp_d)
                                    emit_pair(gp, "win", hf, 1, winp_d)
                                    fmarks[("wout", hf)] = (s_wB, s_wB.inc(gp.dma_start(
                                        out=wout_s[:].rearrange("p (dc e) -> p dc e", dc=8),
                                        in_=wout_d.rearrange("(dc p) e -> p dc e", p=128)),
                                        16))
                                    fmarks[("w2", hf, 0)] = (s_wA, s_wA.inc(gp.dma_start(
                                        out=w2A[:].rearrange("p (dc e) -> p dc e", dc=8),
                                        in_=w2_d[0 * D:(0 + 1) * D, :]
                                            .rearrange("(dc p) e -> p dc e", p=128)), 16))
                                block.gpsimd(_)

                            for pp in range(8):
                                wslice = wpair[:, (pp % 2) * 2048:(pp % 2) * 2048 + 2048]
                                for tg in range(2):
                                    ub = ub_ix[0]
                                    ub_ix[0] = 1 - ub
                                    def _(te, pp=pp, tg=tg, wkey=wkey, hf=hf, wslice=wslice,
                                          ub=ub, puaf=pua_free[ub], pubf=pub_free[ub]):
                                        wp_s, wp_m = fmarks[(wkey, hf, pp)]
                                        te.wait_ge(wp_s.h, wp_m)
                                        te.wait_ge(sa.h, puaf)
                                        te.wait_ge(sv.h, pubf)
                                        for cc in range(8):
                                            mma = te.matmul(
                                                puas[ub][:, 0:512],
                                                wslice[:, cc * 256:cc * 256 + 128],
                                                xlnfF[:, cc * 1024 + tg * 512:
                                                      cc * 1024 + (tg + 1) * 512],
                                                start=(cc == 0), stop=(cc == 7))
                                        st.inc(mma)
                                        for cc in range(8):
                                            mmb = te.matmul(
                                                pubs[ub][:, 0:512],
                                                wslice[:, cc * 256 + 128:cc * 256 + 256],
                                                xlnfF[:, cc * 1024 + tg * 512:
                                                      cc * 1024 + (tg + 1) * 512],
                                                start=(cc == 0), stop=(cc == 7))
                                        st.inc(mmb)
                                    block.tensor(_)
                                    m_a_mm = st.n - 1
                                    m_b_mm = st.n

                                    if tg == 1 and pp + 2 < 8:
                                        def _(gp, wkey=wkey, hf=hf, pp=pp, m_b_mm=m_b_mm,
                                              src_d=wsrc_d):
                                            gp.wait_ge(st.h, m_b_mm)
                                            emit_pair(gp, wkey, hf, pp + 2, src_d)
                                        block.gpsimd(_)

                                    def _(s, ub=ub, m_a_mm=m_a_mm, pubf=pub_free[ub]):
                                        s.wait_ge(st.h, m_a_mm)
                                        s.wait_ge(sv.h, pubf)
                                        sa.inc(s.activation(ssil[:, ub * 512:(ub + 1) * 512],
                                                            puas[ub][:, 0:512], AF.Silu))
                                    block.scalar(_)
                                    pua_free[ub] = sa.n
                                    m_sil = sa.n

                                    def _(v, dst=dst, pp=pp, tg=tg, ub=ub, m_b_mm=m_b_mm,
                                          m_sil=m_sil):
                                        v.wait_ge(st.h, m_b_mm)
                                        v.wait_ge(sa.h, m_sil)
                                        sv.inc(v.tensor_tensor(
                                            dst[:, pp * 1024 + tg * 512:
                                                pp * 1024 + (tg + 1) * 512],
                                            pubs[ub][:, 0:512], ssil[:, ub * 512:(ub + 1) * 512],
                                            OP.mult))
                                    block.vector(_)
                                    pub_free[ub] = sv.n
                            if wkey == "w1":
                                fmarks[("w1mm", hf)] = st.n
                        hid_mark = sv.n

                        psh_free = [sv.n]
                        for trel in range(8):
                            def _(te, trel=trel, hf=hf, hid_mark=hid_mark, pshf=psh_free[0]):
                                wo_s, wo_m = fmarks[("wout", hf)]
                                te.wait_ge(sv.h, hid_mark)
                                te.wait_ge(wo_s.h, wo_m)
                                te.wait_ge(sv.h, pshf)
                                for j in range(2):
                                    for jp in range(8):
                                        mm = te.matmul(
                                            psh[:, j * 512:(j + 1) * 512],
                                            hsh[:, jp * 1024 + trel * 128:
                                                jp * 1024 + (trel + 1) * 128],
                                            wout_s[:, jp * 1024 + j * 512:
                                                   jp * 1024 + (j + 1) * 512],
                                            start=(jp == 0), stop=(jp == 7))
                                st.inc(mm)
                            block.tensor(_)
                            m_s = st.n

                            def _(v, trel=trel, m_s=m_s):
                                v.wait_ge(st.h, m_s)
                                v.wait_ge(sv.h, sv.n)
                                sv.inc(v.tensor_copy(accum[:, trel * 1024:(trel + 1) * 1024],
                                                     psh[:, 0:1024]))
                            block.vector(_)
                            psh_free[0] = sv.n
                        fmarks[("fs_done", hf)] = st.n

                if stage < 5:
                    continue
                with ExitStack() as ps:
                    pr0 = ps.enter_context(nc.psum_tensor(f"pr0_{hf}", [128, 1024], F32))
                    pr1 = ps.enter_context(nc.psum_tensor(f"pr1_{hf}", [128, 1024], F32))
                    prs = [pr0, pr1]
                    do_f1ab_h1 = False
                    if do_f1ab_h1:
                        ptrF2 = ps.enter_context(nc.psum_tensor("ptrF2", [128, 1024], BF16))
                        pgF2 = ps.enter_context(nc.psum_tensor("pgF2", [128, 8], F32))

                    with nc.Block() as block:
                        n_cons = [0]

                        def consume_h1(upto):
                            while do_f1ab_h1 and n_cons[0] < upto:
                                f1ab_steps[1][n_cons[0]](block, ptrF2, pgF2)
                                n_cons[0] += 1

                        last_mm_of_expert = {}
                        for e in range(8):
                            slot = w2A if (e % 2 == 0) else wout_s
                            if e != 0:
                                def _(gp, e=e, hf=hf, slot=slot):
                                    if e >= 2:
                                        gp.wait_ge(st.h, last_mm_of_expert[e - 2])
                                    sw2 = s_wA if e % 2 == 0 else s_wB
                                    fmarks[("w2", hf, e)] = (sw2, sw2.inc(gp.dma_start(
                                        out=slot[:].rearrange("p (dc e2) -> p dc e2", dc=8),
                                        in_=w2_d[e * D:(e + 1) * D, :]
                                            .rearrange("(dc p) e2 -> p dc e2", p=128)), 16))
                                block.gpsimd(_)

                            pr_free = {0: sv.n, 1: sv.n}
                            for trel in range(8):
                                pb = trel % 2
                                def _(te, e=e, trel=trel, pb=pb, hf=hf, slot=slot,
                                      prf=pr_free[pb], hid_mark=hid_mark):
                                    w2_s, w2_m = fmarks[("w2", hf, e)]
                                    te.wait_ge(w2_s.h, w2_m)
                                    te.wait_ge(sv.h, prf)
                                    te.wait_ge(sv.h, hid_mark)
                                    for j in range(2):
                                        for jp in range(8):
                                            mm = te.matmul(
                                                prs[pb][:, j * 512:(j + 1) * 512],
                                                hid[:, jp * 1024 + trel * 128:
                                                    jp * 1024 + (trel + 1) * 128],
                                                slot[:, jp * 1024 + j * 512:
                                                     jp * 1024 + (j + 1) * 512],
                                                start=(jp == 0), stop=(jp == 7))
                                    st.inc(mm)
                                block.tensor(_)
                                m_r = st.n

                                def _(v, e=e, trel=trel, pb=pb, m_r=m_r):
                                    v.wait_ge(st.h, m_r)
                                    v.wait_ge(sv.h, sv.n)
                                    sv.inc(v.scalar_tensor_tensor(
                                        accum[:, trel * 1024:(trel + 1) * 1024],
                                        prs[pb][:, 0:1024],
                                        wdense[:, hf * 64 + trel * 8 + e:
                                                hf * 64 + trel * 8 + e + 1],
                                        accum[:, trel * 1024:(trel + 1) * 1024],
                                        OP.mult, OP.add))
                                block.vector(_)
                                pr_free[pb] = sv.n
                                if e == 7 and trel == 3:
                                    ln4c = [emit_ln_chain(
                                        block,
                                        [(accum[:, r * 1024:r * 1024 + 512],
                                          accum[:, r * 1024 + 512:(r + 1) * 1024])
                                         for r in range(4)], 0)]
                            last_mm_of_expert[e] = st.n
                            consume_h1((e + 1) * 27 // 8)

                        consume_h1(27)

                        res_marks = []
                        for half_grp in range(2):
                            if half_grp == 0:
                                m_chain = ln4c[0]
                            else:
                                srcs4 = [(accum[:, trel * 1024:trel * 1024 + 512],
                                          accum[:, trel * 1024 + 512:(trel + 1) * 1024])
                                         for trel in range(4, 8)]
                                m_chain = emit_ln_chain(block, srcs4, 4)
                            for k in range(QT):
                                trel = half_grp * 4 + k
                                t = T0 + trel
                                slot = trel % 2
                                m4 = emit_apply(block, trel,
                                                accum[:, trel * 1024:(trel + 1) * 1024],
                                                sapply[:, slot * D:(slot + 1) * D],
                                                sap_war[slot], m_chain)

                                def _(v, t=t, slot=slot):
                                    v.wait_ge(sv.h, sv.n)
                                    sv.inc(v.tensor_tensor(
                                        x_res[:, t * D:(t + 1) * D],
                                        x_res[:, t * D:(t + 1) * D],
                                        sapply[:, slot * D:(slot + 1) * D], OP.add))
                                block.vector(_)
                                sap_war[slot] = ("v", sv.n)
                                res_marks.append(sv.n)

                        def _(sy, hf=hf, m=res_marks[-1]):
                            sy.wait_ge(sv.h, m)
                            sd_o.inc(sy.dma_start(
                                out=y_d.rearrange("(t p) f -> p t f", p=128)[
                                    :, hf * 8:(hf + 1) * 8, :],
                                in_=x_res[:, hf * 8 * D:(hf + 1) * 8 * D]
                                    .rearrange("p (t f) -> p t f", t=8)), 16)
                        block.sync(_)

            if debug:
                with nc.Block() as block:
                    def _(sy):
                        sy.wait_ge(sv.h, sv.n)
                        sd_o.inc(sy.dma_start(out=dbg["wdense"], in_=wdense[:]), 16)
                        sd_o.inc(sy.dma_start(out=dbg["hid"], in_=hid[:]), 16)
                        sd_o.inc(sy.dma_start(out=dbg["hsh"], in_=hsh[:]), 16)
                        sd_o.inc(sy.dma_start(out=dbg["accum"], in_=accum[:]), 16)
                        sy.wait_ge(sd_o.h, sd_o.n)
                    block.sync(_)
            else:
                with nc.Block() as block:
                    def _(sy):
                        sy.wait_ge(sd_o.h, sd_o.n)
                    block.sync(_)
        else:
            with nc.Block() as block:
                def _(sy):
                    sy.wait_ge(sd_o.h, sd_o.n)
                block.sync(_)

    return nc


def make_inputs(inputs):
    x = np.ascontiguousarray(inputs["L0_attn_x"], dtype=np.float32)
    wqT = np.ascontiguousarray(inputs["L0_attn_wq_w"].T)
    wkT = np.ascontiguousarray(inputs["L0_attn_wk_w"].T)
    def packch(w):
        return np.ascontiguousarray(
            w.reshape(8, 128, 8, 128).transpose(2, 1, 0, 3).reshape(1024, 1024))
    wkp = packch(wkT)
    wqp = packch(wqT)
    wvT = np.ascontiguousarray(inputs["L0_attn_wv_w"].T)
    woT = np.ascontiguousarray(inputs["L0_attn_wo_w"].T)
    perm = np.concatenate([np.r_[p * 128:(p + 1) * 128, (p + 8) * 128:(p + 9) * 128]
                           for p in range(8)])
    w1p = np.ascontiguousarray(inputs["L0_ffn_w1_shared"][:, perm])
    winp = np.ascontiguousarray(inputs["L0_ffn_w_shared_in"][:, perm])
    wout = np.ascontiguousarray(inputs["L0_ffn_w_shared_out"])
    wgate = np.ascontiguousarray(inputs["L0_ffn_w_gate"])
    w2 = np.ascontiguousarray(inputs["L0_ffn_w2"].reshape(8 * 1024, 1024))
    bqc = np.ascontiguousarray(inputs["L0_attn_wq_b"].reshape(8, 128).T)
    bkc = np.ascontiguousarray(inputs["L0_attn_wk_b"].reshape(8, 128).T)
    bvr = np.ascontiguousarray(inputs["L0_attn_wv_b"].reshape(1, 1024))
    bor = np.ascontiguousarray(inputs["L0_attn_wo_b"].reshape(1, 1024))
    gbr = np.ascontiguousarray(inputs["L0_ffn_gate_bias"].reshape(1, 8))
    ident = np.eye(128, dtype=BF)
    maskT = np.triu(np.ones((128, 128), np.float32)).astype(BF)
    onesr = np.ones((1, 128), dtype=BF)
    magici = np.full((128, 8), MAGIC, np.int32)

    in_maps = []
    for c in range(8):
        b, hf = c // 2, c % 2
        sel = np.zeros((128, 8), np.float32)
        sel[:, 0] = float(hf)
        in_maps.append(dict(
            x=np.ascontiguousarray(x[b, hf * 2048:(hf + 1) * 2048]),
            wqT=wqT, wkT=wkT, wkp=wkp, wqp=wqp, wvT=wvT, woT=woT,
            w1p=w1p, winp=winp, wout=wout, wgate=wgate, w2=w2,
            bqc=bqc, bkc=bkc, bvr=bvr, bor=bor, gbr=gbr,
            ident=ident, maskT=maskT, onesr=onesr, sel=sel, magici=magici,
        ))
    return in_maps


def kernel(**inputs):
    in_maps = make_inputs(inputs)
    nc = build(debug=False)
    res = run_bass_kernel_spmd(nc, in_maps, core_ids=list(range(8)))
    out = np.empty((4, 4096, 1024), np.float32)
    for c in range(8):
        b, hf = c // 2, c % 2
        out[b, hf * 2048:(hf + 1) * 2048] = res.results[c]["y"]
    return out


# version 15
# speedup vs baseline: 1.0016x; 1.0016x over previous
import numpy as np
import ml_dtypes
import concourse.bass as bass
import concourse.mybir as mybir
from concourse.alu_op_type import AluOpType as OP
from concourse.bass_utils import run_bass_kernel_spmd
from contextlib import ExitStack

F32 = mybir.dt.float32
BF16 = mybir.dt.bfloat16
I32 = mybir.dt.int32
AF = mybir.ActivationFunctionType
AX = mybir.AxisListType
BF = ml_dtypes.bfloat16

D = 1024
NH = 16
TOK = 2048
NT = 16
QT = 4
EPS = 1e-5
NEG = -1e30
MAGIC = 0x5F3759DF


class Sem:
    def __init__(self, ctx, nc, name):
        self.h = ctx.enter_context(nc.semaphore(name))
        self.n = 0

    def inc(self, inst, k=1):
        inst.then_inc(self.h, k)
        self.n += k
        return self.n


def build(debug=False, stage=99, a2sub=6):
    nc = bass.Bass("TRN2", target_bir_lowering=False, debug=False)

    dt = lambda name, shape, dty=F32: nc.dram_tensor(name, shape, dty, kind="ExternalInput").ap()
    x_d = dt("x", [TOK, D])
    wqT_d = dt("wqT", [D, D])
    wkT_d = dt("wkT", [D, D])
    wkp_d = dt("wkp", [D, D])
    wqp_d = dt("wqp", [D, D])
    wvT_d = dt("wvT", [D, D])
    woT_d = dt("woT", [D, D])
    w1p_d = dt("w1p", [D, 2 * D])
    winp_d = dt("winp", [D, 2 * D])
    wout_d = dt("wout", [D, D])
    wgate_d = dt("wgate", [D, 8])
    w2_d = dt("w2", [8 * D, D])
    bq_d = dt("bqc", [128, 8])
    bk_d = dt("bkc", [128, 8])
    bv_d = dt("bvr", [1, D])
    bo_d = dt("bor", [1, D])
    gb_d = dt("gbr", [1, 8])
    ident_d = nc.dram_tensor("ident", [128, 128], BF16, kind="ExternalInput").ap()
    mask_d = nc.dram_tensor("maskT", [128, 128], BF16, kind="ExternalInput").ap()
    ones_d = nc.dram_tensor("onesr", [1, 128], BF16, kind="ExternalInput").ap()
    sel_d = dt("sel", [128, 8])
    magic_d = nc.dram_tensor("magici", [128, 8], I32, kind="ExternalInput").ap()

    y_d = nc.dram_tensor("y", [TOK, D], F32, kind="ExternalOutput").ap()
    dbg = {}
    if debug:
        dbg["srun"] = nc.dram_tensor("dbg_srun", [128, 512], F32, kind="ExternalOutput").ap()
        dbg["sbase"] = nc.dram_tensor("dbg_sbase", [128, 512], F32, kind="ExternalOutput").ap()
        dbg["xattn"] = nc.dram_tensor("dbg_xattn", [TOK, D], F32, kind="ExternalOutput").ap()
        dbg["wdense"] = nc.dram_tensor("dbg_wdense", [128, 128], F32, kind="ExternalOutput").ap()
        dbg["hid"] = nc.dram_tensor("dbg_hid", [128, 8 * 1024], BF16, kind="ExternalOutput").ap()
        dbg["hsh"] = nc.dram_tensor("dbg_hsh", [128, 8 * 1024], BF16, kind="ExternalOutput").ap()
        dbg["accum"] = nc.dram_tensor("dbg_accum", [128, 8 * 1024], F32, kind="ExternalOutput").ap()
        dbg["xres"] = nc.dram_tensor("dbg_xres", [TOK, D], F32, kind="ExternalOutput").ap()

    cc_in = nc.dram_tensor("cc_in", [128, 512], F32)
    cc_out = nc.dram_tensor("cc_out", [256, 512], F32)

    with ExitStack() as g:
        eg = g.enter_context
        x_res = eg(nc.sbuf_tensor("x_res", [128, NT * D], F32))
        sapply = eg(nc.sbuf_tensor("sapply", [128, 2 * D], BF16))
        statsr = eg(nc.sbuf_tensor("statsr", [128, 192], F32))
        lnmv = eg(nc.sbuf_tensor("lnmv", [128, 32], F32))
        lnu = eg(nc.sbuf_tensor("lnu", [128, 16], F32))
        lnh = eg(nc.sbuf_tensor("lnh", [128, 16], F32))
        lnz = eg(nc.sbuf_tensor("lnz", [128, 16], F32))
        lnt = eg(nc.sbuf_tensor("lnt", [128, 16], F32))
        lnb = eg(nc.sbuf_tensor("lnb", [128, 16], F32))
        st2 = eg(nc.sbuf_tensor("st2", [128, 24], F32))
        smag = eg(nc.sbuf_tensor("smag", [128, 8], I32))
        ident = eg(nc.sbuf_tensor("ident_s", [128, 128], BF16))
        onesr = eg(nc.sbuf_tensor("onesr_s", [1, 128], BF16))
        bvr = eg(nc.sbuf_tensor("bvr_s", [1, D], BF16))
        bor = eg(nc.sbuf_tensor("bor_s", [1, D], BF16))
        gbr = eg(nc.sbuf_tensor("gbr_s", [1, 8], BF16))
        bqc = eg(nc.sbuf_tensor("bqc_s", [128, 8], F32))
        bkc = eg(nc.sbuf_tensor("bkc_s", [128, 8], F32))
        wgate_b = eg(nc.sbuf_tensor("wgate_b", [128, 64], BF16))
        selb = eg(nc.sbuf_tensor("sel_s", [128, 8], F32))

        sd_x = Sem(g, nc, "sd_x")
        sd_x1 = Sem(g, nc, "sd_x1")
        sd_x2 = Sem(g, nc, "sd_x2")
        sd_x3 = Sem(g, nc, "sd_x3")
        sd_w = Sem(g, nc, "sd_w")
        s_wv = Sem(g, nc, "s_wv")
        s_row = Sem(g, nc, "s_row")
        s_ck0 = Sem(g, nc, "s_ck0")
        s_ck1 = Sem(g, nc, "s_ck1")
        s_wp0 = Sem(g, nc, "s_wp0")
        s_wp1 = Sem(g, nc, "s_wp1")
        s_wA = Sem(g, nc, "s_wA")
        s_wB = Sem(g, nc, "s_wB")
        sd_cc = Sem(g, nc, "sd_cc")
        sd_o = Sem(g, nc, "sd_o")
        st = Sem(g, nc, "st")
        sv = Sem(g, nc, "sv")
        sa = Sem(g, nc, "sa")
        sg = Sem(g, nc, "sg")
        sp = Sem(g, nc, "sp")

        LG, EB, PB, GE, MB = 0, 64, 128, 192, 256
        GMX, GSUM, GRS, GM1, GM2 = 320, 328, 336, 344, 352

        sap_war = [("t", 0), ("t", 0)]

        def wait_war(eng, war):
            kind, mark = war
            if mark:
                eng.wait_ge(st.h if kind == "t" else sv.h, mark)

        def emit_ln_chain(block, srcs, base, extra_wait_t=None, extra_wait_a=None):
            n = len(srcs)
            b = base

            def _(v):
                if extra_wait_t is not None:
                    v.wait_ge(st.h, extra_wait_t)
                if extra_wait_a is not None:
                    v.wait_ge(sa.h, extra_wait_a)
                v.wait_ge(sv.h, sv.n)

                def step(inst):
                    sv.inc(inst)
                    v.wait_ge(sv.h, sv.n)

                mm = None
                for i, (p1, p2) in enumerate(srcs):
                    v.bn_stats(statsr[:, (b + i) * 12:(b + i) * 12 + 6], p1)
                    mm = v.bn_stats(statsr[:, (b + i) * 12 + 6:(b + i) * 12 + 12], p2)
                step(mm)
                for i in range(n):
                    mm = v.bn_aggr(lnmv[:, (b + i) * 2:(b + i + 1) * 2],
                                   statsr[:, (b + i) * 12:(b + i + 1) * 12])
                step(mm)
                mv3 = lnmv[:].rearrange("p (i two) -> p two i", two=2)
                var_v = mv3[:, 1:2, b:b + n]
                mean_v = mv3[:, 0:1, b:b + n]
                step(v.tensor_scalar(lnu[:, b:b + n], var_v, EPS, None, OP.add))
                step(v.tensor_scalar(lnz[:, b:b + n].bitcast(I32),
                                     lnu[:, b:b + n].bitcast(I32), 1,
                                     None, OP.arith_shift_right))
                step(v.tensor_tensor(lnz[:, b:b + n].bitcast(I32), smag[:, 0:n],
                                     lnz[:, b:b + n].bitcast(I32), OP.subtract))
                step(v.tensor_scalar(lnh[:, b:b + n], lnu[:, b:b + n], 0.5, None, OP.mult))
                for _i in range(2):
                    step(v.tensor_tensor(lnt[:, b:b + n], lnz[:, b:b + n], lnz[:, b:b + n],
                                         OP.mult))
                    step(v.tensor_tensor(lnt[:, b:b + n], lnt[:, b:b + n], lnh[:, b:b + n],
                                         OP.mult))
                    step(v.tensor_scalar(lnt[:, b:b + n], lnt[:, b:b + n], -1.0, 1.5,
                                         OP.mult, OP.add))
                    step(v.tensor_tensor(lnz[:, b:b + n], lnz[:, b:b + n], lnt[:, b:b + n],
                                         OP.mult))
                step(v.scalar_tensor_tensor(lnb[:, b:b + n], mean_v, -1.0, lnz[:, b:b + n],
                                            OP.mult, OP.mult))
            block.vector(_)
            return sv.n

        def emit_apply(block, i, asrc, dst, war, m_chain):
            def _(v, i=i, asrc=asrc, dst=dst, war=war):
                wait_war(v, war)
                v.wait_ge(sv.h, sv.n)
                sv.inc(v.tensor_scalar(dst, asrc, lnz[:, i:i + 1], lnb[:, i:i + 1],
                                       OP.mult, OP.add))
            block.vector(_)
            return sv.n

        def emit_ln2(block, p1, p2, asrc, dst, war, extra_wait_t):
            def _(v):
                v.wait_ge(st.h, extra_wait_t)
                v.wait_ge(sv.h, sv.n)

                def step(inst):
                    sv.inc(inst)
                    v.wait_ge(sv.h, sv.n)

                v.bn_stats(st2[:, 0:6], p1)
                step(v.bn_stats(st2[:, 6:12], p2))
                step(v.bn_aggr(st2[:, 12:14], st2[:, 0:12]))
                step(v.tensor_scalar(st2[:, 14:15], st2[:, 13:14], EPS, None, OP.add))
                step(v.tensor_scalar(st2[:, 16:17].bitcast(I32), st2[:, 14:15].bitcast(I32),
                                     1, None, OP.arith_shift_right))
                step(v.tensor_tensor(st2[:, 16:17].bitcast(I32), smag[:, 0:1],
                                     st2[:, 16:17].bitcast(I32), OP.subtract))
                step(v.tensor_scalar(st2[:, 15:16], st2[:, 14:15], 0.5, None, OP.mult))
                for _i in range(2):
                    step(v.tensor_tensor(st2[:, 17:18], st2[:, 16:17], st2[:, 16:17], OP.mult))
                    step(v.tensor_tensor(st2[:, 17:18], st2[:, 17:18], st2[:, 15:16], OP.mult))
                    step(v.tensor_scalar(st2[:, 17:18], st2[:, 17:18], -1.0, 1.5,
                                         OP.mult, OP.add))
                    step(v.tensor_tensor(st2[:, 16:17], st2[:, 16:17], st2[:, 17:18], OP.mult))
                step(v.scalar_tensor_tensor(st2[:, 18:19], st2[:, 12:13], -1.0, st2[:, 16:17],
                                            OP.mult, OP.mult))
            block.vector(_)
            m_chain = sv.n

            def _(s, asrc=asrc, dst=dst, war=war, m_chain=m_chain):
                s.wait_ge(sv.h, m_chain)
                wait_war(s, war)
                sa.inc(s.activation(dst, asrc, AF.Identity,
                                    scale=st2[:, 16:17], bias=st2[:, 18:19]))
            block.scalar(_)
            return sa.n

        with ExitStack() as ascope:
            ea = ascope.enter_context
            kfm = ea(nc.sbuf_tensor("kfm", [128, 8 * TOK], BF16))
            ktm_buf = ea(nc.sbuf_tensor("ktm", [128, 1024], BF16))
            ktm_buf2 = ea(nc.sbuf_tensor("ktm2", [128, 1024], BF16))
            vtm = ea(nc.sbuf_tensor("vtm", [128, NT * D], BF16))
            snap = ea(nc.sbuf_tensor("snap", [128, NT * 512], BF16))
            srun = ea(nc.sbuf_tensor("srun", [128, 512], F32))
            sbase = ea(nc.sbuf_tensor("sbase", [128, 512], F32))
            maskT = ea(nc.sbuf_tensor("maskT_s", [128, 128], BF16))
            xlnf = ea(nc.sbuf_tensor("xlnf", [128, 8 * 512], BF16))
            qfm = ea(nc.sbuf_tensor("qfm", [128, 8 * 512], BF16))
            sst = ea(nc.sbuf_tensor("sst", [128, 1024], BF16))
            sattn = ea(nc.sbuf_tensor("sattn", [128, 1024], BF16))
            ybuf = ea(nc.sbuf_tensor("ybuf", [128, 2048], BF16))
            wbig = ea(nc.sbuf_tensor("wbig", [128, 8 * D], BF16))
            wdb = ea(nc.sbuf_tensor("wdb", [128, 2 * 1024], BF16))

            wv_mark = [0]
            wo_mark = [0]
            wchunk_marks = {}

            with nc.Block() as block:
                def _(sy):
                    sd_x.inc(sy.dma_start(out=ident[:], in_=ident_d), 16)
                    sd_x.inc(sy.dma_start(out=maskT[:], in_=mask_d), 16)
                    sd_x.inc(sy.dma_start(out=onesr[:], in_=ones_d), 16)
                    sd_x.inc(sy.dma_start(out=bqc[:], in_=bq_d), 16)
                    sd_x.inc(sy.dma_start(out=bkc[:], in_=bk_d), 16)
                    sd_x.inc(sy.dma_start(out=selb[:], in_=sel_d), 16)
                    sd_x.inc(sy.dma_start(out=smag[:], in_=magic_d), 16)
                    xq_sems = (sd_x, sd_x1, sd_x2, sd_x3)
                    for xq in range(4):
                        xq_sems[xq].inc(sy.dma_start(
                            out=x_res[:, xq * QT * D:(xq + 1) * QT * D]
                                .rearrange("p (t f) -> p t f", t=QT),
                            in_=x_d[xq * QT * 128:(xq + 1) * QT * 128, :]
                                .rearrange("(t p) f -> p t f", p=128)), 16)
                block.sync(_)
                x_mark = sd_x.n

                def _(gp):
                    wv_mark[0] = s_wv.inc(gp.dma_start(
                        out=wbig[:].rearrange("p (dc e) -> p dc e", dc=8),
                        in_=wvT_d.rearrange("(dc p) e -> p dc e", p=128)), 16)
                    s_row.inc(gp.dma_start(out=bvr[:], in_=bv_d), 16)
                    s_row.inc(gp.dma_start(out=bor[:], in_=bo_d), 16)
                    s_row.inc(gp.dma_start(out=gbr[:], in_=gb_d), 16)
                    s_row.inc(gp.dma_start(
                        out=wgate_b[:].rearrange("p (dc e) -> p dc e", dc=8),
                        in_=wgate_d.rearrange("(dc p) e -> p dc e", p=128)), 16)
                block.gpsimd(_)
                row_mark = s_row.n

                def _(v):
                    sv.inc(v.memset(srun[:], 0.0))
                block.vector(_)

            def emit_wchunk(gp, wsrc, buf_idx, key):
                ec = key[1]
                sck = s_ck0 if buf_idx == 0 else s_ck1
                wchunk_marks[key] = (sck, sck.inc(gp.dma_start(
                    out=wdb[:, buf_idx * 1024:(buf_idx + 1) * 1024],
                    in_=wsrc[ec * 128:(ec + 1) * 128, :]), 16))

            def emit_transpose(block, ptr, ptr_free, slot, dstview, apply_mark):
                def _(te, slot=slot, apply_mark=apply_mark, pf=ptr_free[0]):
                    te.wait_ge(sv.h, apply_mark)
                    te.wait_ge(sv.h, pf)
                    for cc in range(8):
                        mm = te.transpose(ptr[:, cc * 128:(cc + 1) * 128],
                                          sapply[:, slot * D + cc * 128:
                                                 slot * D + (cc + 1) * 128],
                                          ident[:])
                    st.inc(mm)
                block.tensor(_)
                m_tr = st.n

                def _(v, m_tr=m_tr, dstview=dstview):
                    v.wait_ge(st.h, m_tr)
                    sv.inc(v.tensor_copy(dstview,
                                         ptr[:].rearrange("p (c w) -> p c w", c=8)))
                block.vector(_)
                ptr_free[0] = sv.n
                return m_tr, sv.n

            if stage >= 1:
              with ExitStack() as ps:
                ptr = ps.enter_context(nc.psum_tensor("ptrA1", [128, 1024], BF16))
                pk0 = ps.enter_context(nc.psum_tensor("pk0", [128, 512], F32))
                pk1 = ps.enter_context(nc.psum_tensor("pk1", [128, 512], F32))
                pv0 = ps.enter_context(nc.psum_tensor("pv0", [128, 1024], F32))
                pv1 = ps.enter_context(nc.psum_tensor("pv1", [128, 1024], F32))
                pds = ps.enter_context(nc.psum_tensor("pds", [128, 512], F32))
                pks = [pk0, pk1]
                pvs = [pv0, pv1]

                with nc.Block() as block:
                    def _(gp):
                        emit_wchunk(gp, wkp_d, 0, ("k", 0, 0))
                        emit_wchunk(gp, wkp_d, 1, ("k", 1, 0))
                    block.gpsimd(_)

                    ptr_free = [sv.n]
                    pv_free = {0: sv.n, 1: sv.n}
                    pds_free = [sv.n]
                    kevac = {}
                    k_evac_all = []
                    vtm_marks = {}
                    ktm_evac = [0]
                    snap_cast = [0]
                    pend_ds = []
                    ktm_bufs = {0: ktm_buf[:, 0:1024], 1: ktm_buf2[:, 0:1024]}

                    def a_srcs(q):
                        return [(x_res[:, t * D:t * D + 512],
                                 x_res[:, t * D + 512:(t + 1) * D])
                                for t in range(q * QT, (q + 1) * QT)]

                    chain_marks = {}
                    def _(v):
                        v.wait_ge(sd_x.h, x_mark)
                    block.vector(_)
                    chain_marks[0] = emit_ln_chain(block, a_srcs(0), 0)

                    for q in range(4):
                        for trel in range(QT):
                            t = q * QT + trel
                            slot = trel % 2
                            m_ap = emit_apply(block, q * QT + trel,
                                              x_res[:, t * D:(t + 1) * D],
                                              sapply[:, slot * D:(slot + 1) * D],
                                              sap_war[slot], chain_marks[q])
                            dstv = xlnf[:].rearrange("p (c w) -> p c w", c=8)[
                                :, :, trel * 128:(trel + 1) * 128]
                            m_tr, _mv = emit_transpose(block, ptr, ptr_free, slot, dstv,
                                                       m_ap)
                            sap_war[slot] = ("t", m_tr)
                        xlnf_mark = sv.n
                        if q + 1 < 4:
                            def _(v, q=q):
                                v.wait_ge((sd_x1, sd_x2, sd_x3)[q].h, 16)
                            block.vector(_)
                            chain_marks[q + 1] = emit_ln_chain(block, a_srcs(q + 1),
                                                               (q + 1) * QT)

                        for ec in range(8):
                            seq = q * 8 + ec
                            def _(te, ec=ec, q=q, seq=seq, xlnf_mark=xlnf_mark):
                                ck_s, ck_m = wchunk_marks[("k", ec, q)]
                                te.wait_ge(ck_s.h, ck_m)
                                te.wait_ge(sv.h, xlnf_mark)
                                if len(k_evac_all) >= 2:
                                    te.wait_ge(sv.h, k_evac_all[-2])
                                for cc in range(8):
                                    mm = te.matmul(pks[seq % 2][:, 0:512],
                                                   wdb[:, (seq % 2) * 1024 + cc * 128:
                                                       (seq % 2) * 1024 + (cc + 1) * 128],
                                                   xlnf[:, cc * 512:(cc + 1) * 512],
                                                   start=(cc == 0), stop=(cc == 7))
                                st.inc(mm)
                            block.tensor(_)
                            m_k = st.n

                            if seq + 2 < 32:
                                def _(gp, seq=seq, m_k=m_k):
                                    gp.wait_ge(st.h, m_k)
                                    q2, ec2 = divmod(seq + 2, 8)
                                    emit_wchunk(gp, wkp_d, (seq + 2) % 2, ("k", ec2, q2))
                                block.gpsimd(_)

                            def _(v, ec=ec, q=q, seq=seq, m_k=m_k):
                                v.wait_ge(st.h, m_k)
                                sv.inc(v.tensor_scalar(
                                    kfm[:, ec * TOK + q * 512:ec * TOK + (q + 1) * 512],
                                    pks[seq % 2][:, 0:512],
                                    bkc[:, ec:ec + 1], None, OP.add))
                            block.vector(_)
                            k_evac_all.append(sv.n)
                        kevac[q] = sv.n

                        for trel in range(QT):
                            t = q * QT + trel
                            pb = t % 2
                            def _(te, trel=trel, t=t, pb=pb, pvf=pv_free[pb]):
                                te.wait_ge(s_wv.h, wv_mark[0])
                                te.wait_ge(s_row.h, row_mark)
                                te.wait_ge(sv.h, pvf)
                                for j in range(2):
                                    for cc in range(8):
                                        te.matmul(pvs[pb][:, j * 512:(j + 1) * 512],
                                                  xlnf[:, cc * 512 + trel * 128:
                                                       cc * 512 + (trel + 1) * 128],
                                                  wbig[:, cc * D + j * 512:cc * D + (j + 1) * 512],
                                                  start=(cc == 0), stop=False)
                                    mm = te.matmul(pvs[pb][:, j * 512:(j + 1) * 512],
                                                   onesr[:],
                                                   bvr[0:1, j * 512:(j + 1) * 512],
                                                   start=False, stop=True)
                                st.inc(mm)
                            block.tensor(_)
                            m_vmm = st.n

                            def _(v, t=t, pb=pb, m_vmm=m_vmm):
                                v.wait_ge(st.h, m_vmm)
                                sv.inc(v.tensor_copy(vtm[:, t * D:(t + 1) * D],
                                                     pvs[pb][:, 0:1024]))
                            block.vector(_)
                            pv_free[pb] = sv.n
                            vtm_marks[t] = sv.n

                            def _(te, t=t, q=q, pf=ptr_free[0]):
                                te.wait_ge(sv.h, kevac[q])
                                te.wait_ge(sv.h, pf)
                                for ec in range(8):
                                    mm = te.transpose(
                                        ptr[:, ec * 128:(ec + 1) * 128],
                                        kfm[:, ec * TOK + t * 128:ec * TOK + (t + 1) * 128],
                                        ident[:])
                                st.inc(mm)
                            block.tensor(_)
                            m_ktr = st.n

                            def _(v, t=t, m_ktr=m_ktr):
                                v.wait_ge(st.h, m_ktr)
                                sv.inc(v.tensor_copy(ktm_bufs[t % 2], ptr[:, 0:1024]))
                            block.vector(_)
                            ptr_free[0] = sv.n
                            ktm_evac[0] = sv.n

                            def emit_ds(tds, ke, kb):
                                def _(te, tds=tds, ke=ke, kb=kb, pdf=pds_free[0]):
                                    te.wait_ge(sv.h, ke)
                                    te.wait_ge(sv.h, vtm_marks[tds])
                                    te.wait_ge(sv.h, pdf)
                                    for h in range(NH):
                                        rb = (h % 2) * 64
                                        cb = (h // 2) * 64
                                        mm = te.matmul(
                                            pds[rb:rb + 64, cb:cb + 64],
                                            kb[:, (h // 2) * 128 + (h % 2) * 64:
                                               (h // 2) * 128 + (h % 2) * 64 + 64],
                                            vtm[:, tds * D + h * 64:tds * D + (h + 1) * 64],
                                            start=True, stop=True)
                                    st.inc(mm)
                                block.tensor(_)
                                m_ds = st.n

                                def _(v, tds=tds, m_ds=m_ds):
                                    v.wait_ge(st.h, m_ds)
                                    v.wait_ge(sv.h, sv.n)
                                    sv.inc(v.tensor_tensor(srun[:], pds[0:128, 0:512],
                                                           srun[:], OP.add))
                                    v.wait_ge(sv.h, sv.n)
                                    sv.inc(v.tensor_copy(snap[:, tds * 512:(tds + 1) * 512],
                                                         srun[:]))
                                block.vector(_)
                                pds_free[0] = sv.n

                            pend_ds.append((t, ktm_evac[0], ktm_bufs[t % 2]))
                            if len(pend_ds) > 1:
                                emit_ds(*pend_ds.pop(0))

                    while pend_ds:
                        emit_ds(*pend_ds.pop(0))

            def emit_cc_gp(block):
                a1_a = sa.n
                a1_v = sv.n

                def _(gp):
                    gp.wait_ge(sa.h, a1_a)
                    gp.wait_ge(sv.h, a1_v)
                    m = sd_cc.inc(gp.dma_start(out=cc_in.ap(), in_=srun[:]), 16)
                    if stage >= 3:
                        emit_wchunk(gp, wqp_d, 0, ("q", 0, 0))
                        emit_wchunk(gp, wqp_d, 1, ("q", 1, 0))
                    gp.wait_ge(sd_cc.h, m)
                    sg.inc(gp.collective_compute(
                        "AllGather", OP.bypass,
                        replica_groups=[[0, 1], [2, 3], [4, 5], [6, 7]],
                        ins=[cc_in.ap().opt()],
                        outs=[cc_out.ap().opt()]))
                    gp.wait_ge(sg.h, sg.n)
                    wo_mark[0] = s_wv.inc(gp.dma_start(
                        out=wbig[:].rearrange("p (dc e) -> p dc e", dc=8),
                        in_=woT_d.rearrange("(dc p) e -> p dc e", p=128)), 16)
                    m2 = sd_cc.inc(gp.dma_start(out=sbase[:], in_=cc_out.ap()[0:128, :]), 16)
                    gp.wait_ge(sd_cc.h, m2)
                block.gpsimd(_)

                def _(v):
                    v.wait_ge(sd_cc.h, sd_cc.n)
                    v.wait_ge(sd_x.h, x_mark)
                    sv.inc(v.tensor_scalar(sbase[:], sbase[:], selb[:, 0:1], None, OP.mult))
                    v.wait_ge(sv.h, sv.n)
                block.vector(_)
                return sv.n

            def emit_fold(block, pool_mark):
                def _(v, pool_mark=pool_mark):
                    v.wait_ge(sv.h, pool_mark)
                    for t in range(NT - 1, 0, -1):
                        sv.inc(v.tensor_tensor(snap[:, t * 512:(t + 1) * 512],
                                               sbase[:],
                                               snap[:, (t - 1) * 512:t * 512], OP.add))
                        v.wait_ge(sv.h, sv.n)
                    sv.inc(v.tensor_copy(snap[:, 0:512], sbase[:]))
                block.vector(_)
                return sv.n

            fold_mark_g = [0]
            if stage >= 2:
                with nc.Block() as block:
                    pm = emit_cc_gp(block)
                    fold_mark_g[0] = emit_fold(block, pm)
                    if debug:
                        def _(sy):
                            sy.wait_ge(sv.h, sv.n)
                            sd_o.inc(sy.dma_start(out=dbg["srun"], in_=srun[:]), 16)
                            sd_o.inc(sy.dma_start(out=dbg["sbase"], in_=sbase[:]), 16)
                        block.sync(_)

            if stage >= 3:
              with ExitStack() as ps:
                ptr = ps.enter_context(nc.psum_tensor("ptrA2", [128, 1024], BF16))
                pq0 = ps.enter_context(nc.psum_tensor("pq0", [128, 512], F32))
                pq1 = ps.enter_context(nc.psum_tensor("pq1", [128, 512], F32))
                psc = ps.enter_context(nc.psum_tensor("psc", [128, 1024], F32))
                pattn = ps.enter_context(nc.psum_tensor("pattn", [128, 512], F32))
                py = ps.enter_context(nc.psum_tensor("py", [128, 1024], F32))
                pqs = [pq0, pq1]

                with nc.Block() as block:
                    fold_mark = [fold_mark_g[0]]

                    ptr_free = [sv.n]
                    py_free = [("v", 0)]
                    ybuf_war = [0]
                    ln2war = {0: 0, 1: 0}
                    q_evac_all = []
                    psc_free = [sv.n]
                    pattn_free = [sv.n]

                    def a_srcs(q):
                        return [(x_res[:, t * D:t * D + 512],
                                 x_res[:, t * D + 512:(t + 1) * D])
                                for t in range(q * QT, (q + 1) * QT)]

                    chain_marks = {}
                    chain_marks[0] = emit_ln_chain(block, a_srcs(0), 0)
                    pend_wo = [None]

                    def emit_wo_stage(t, trel, am):
                        def _(te, m=am, pyf=py_free[0]):
                            te.wait_ge(sv.h, m)
                            te.wait_ge(s_wv.h, wo_mark[0])
                            te.wait_ge(s_row.h, row_mark)
                            wait_war(te, pyf)
                            for j in range(2):
                                for cc in range(8):
                                    te.matmul(py[:, j * 512:(j + 1) * 512],
                                              sattn[:, cc * 128:(cc + 1) * 128],
                                              wbig[:, cc * D + j * 512:cc * D + (j + 1) * 512],
                                              start=(cc == 0), stop=False)
                                mm = te.matmul(py[:, j * 512:(j + 1) * 512],
                                               onesr[:],
                                               bor[0:1, j * 512:(j + 1) * 512],
                                               start=False, stop=True)
                            st.inc(mm)
                        block.tensor(_)
                        m_wo = st.n

                        if a2sub < 6:
                            return
                        sy_ = trel % 2

                        def _(v, m_wo=m_wo, sy_=sy_):
                            v.wait_ge(st.h, m_wo)
                            sv.inc(v.tensor_copy(
                                ybuf[:, sy_ * 1024:(sy_ + 1) * 1024], py[:, 0:1024]))
                        block.vector(_)
                        py_free[0] = ("v", sv.n)

                        if trel % 2 == 1:
                            par = (trel // 2) % 2
                            base2 = par * 2
                            chP = emit_ln_chain(
                                block,
                                [(ybuf[:, 0:512], ybuf[:, 512:1024]),
                                 (ybuf[:, 1024:1536], ybuf[:, 1536:2048])],
                                base2)
                            aps = []
                            res_m = 0
                            for j in range(2):
                                tt = t - 1 + j
                                m_apj = emit_apply(block, base2 + j,
                                                   ybuf[:, j * 1024:(j + 1) * 1024],
                                                   ybuf[:, j * 1024:(j + 1) * 1024],
                                                   ("t", 0), chP)
                                aps.append(m_apj)

                                def _(v, tt=tt, j=j):
                                    v.wait_ge(sv.h, sv.n)
                                    sv.inc(v.tensor_tensor(
                                        x_res[:, tt * D:(tt + 1) * D],
                                        x_res[:, tt * D:(tt + 1) * D],
                                        ybuf[:, j * 1024:(j + 1) * 1024], OP.add))
                                block.vector(_)
                                res_m = sv.n
                            ln2war[par] = aps[-1]
                            ybuf_war[0] = aps[-1]
                            ybuf_res = res_m

                    for q in range(4):
                        for trel in range(QT):
                            t = q * QT + trel
                            slot = trel % 2
                            m_ap = emit_apply(block, q * QT + trel,
                                              x_res[:, t * D:(t + 1) * D],
                                              sapply[:, slot * D:(slot + 1) * D],
                                              sap_war[slot], chain_marks[q])
                            dstv = xlnf[:].rearrange("p (c w) -> p c w", c=8)[
                                :, :, trel * 128:(trel + 1) * 128]
                            m_tr, _mv = emit_transpose(block, ptr, ptr_free, slot, dstv,
                                                       m_ap)
                            sap_war[slot] = ("t", m_tr)
                        xlnf_mark = sv.n
                        if q + 1 < 4:
                            chain_marks[q + 1] = emit_ln_chain(block, a_srcs(q + 1),
                                                               (q + 1) * QT)

                        for ec in range(8 if a2sub >= 2 else 0):
                            seq = q * 8 + ec
                            def _(te, ec=ec, q=q, seq=seq, xlnf_mark=xlnf_mark):
                                ck_s, ck_m = wchunk_marks[("q", ec, q)]
                                te.wait_ge(ck_s.h, ck_m)
                                te.wait_ge(sv.h, xlnf_mark)
                                if len(q_evac_all) >= 2:
                                    te.wait_ge(sv.h, q_evac_all[-2])
                                for cc in range(8):
                                    mm = te.matmul(pqs[seq % 2][:, 0:512],
                                                   wdb[:, (seq % 2) * 1024 + cc * 128:
                                                       (seq % 2) * 1024 + (cc + 1) * 128],
                                                   xlnf[:, cc * 512:(cc + 1) * 512],
                                                   start=(cc == 0), stop=(cc == 7))
                                st.inc(mm)
                            block.tensor(_)
                            m_q = st.n

                            if seq + 2 < 32:
                                def _(gp, seq=seq, m_q=m_q):
                                    gp.wait_ge(st.h, m_q)
                                    q2, ec2 = divmod(seq + 2, 8)
                                    emit_wchunk(gp, wqp_d, (seq + 2) % 2, ("q", ec2, q2))
                                block.gpsimd(_)

                            def _(v, ec=ec, seq=seq, m_q=m_q):
                                v.wait_ge(st.h, m_q)
                                sv.inc(v.tensor_scalar(qfm[:, ec * 512:(ec + 1) * 512],
                                                       pqs[seq % 2][:, 0:512],
                                                       bqc[:, ec:ec + 1], None, OP.add))
                            block.vector(_)
                            q_evac_all.append(sv.n)
                        qdone = sv.n

                        if pend_wo[0] is not None:
                            emit_wo_stage(*pend_wo[0])
                            pend_wo[0] = None

                        for trel in range(QT if a2sub >= 3 else 0):
                            t = q * QT + trel
                            attn_marks = []
                            sstb = [sst, ktm_buf]
                            sst_marks = {}
                            for hg in range(2):
                                def _(te, t=t, trel=trel, hg=hg, qdone=qdone,
                                      pf=psc_free[0]):
                                    te.wait_ge(sv.h, qdone)
                                    te.wait_ge(sv.h, pf)
                                    for hrel in range(8):
                                        h = hg * 8 + hrel
                                        rb = (h % 2) * 64
                                        ec = h // 2
                                        pos = (hrel % 2) * 512 + (hrel // 2) * 128
                                        mm = te.matmul(
                                            psc[:, pos:pos + 128],
                                            kfm[rb:rb + 64, ec * TOK + t * 128:
                                                ec * TOK + (t + 1) * 128],
                                            qfm[rb:rb + 64, ec * 512 + trel * 128:
                                                ec * 512 + (trel + 1) * 128],
                                            start=True, stop=True)
                                    st.inc(mm)
                                block.tensor(_)
                                m_sc = st.n

                                def _(v, hg=hg, m_sc=m_sc):
                                    v.wait_ge(st.h, m_sc)
                                    for hrel in range(8):
                                        pos = (hrel % 2) * 512 + (hrel // 2) * 128
                                        mm = v.tensor_tensor(
                                            sstb[hg][:, pos:pos + 128],
                                            psc[:, pos:pos + 128],
                                            maskT[:], OP.mult)
                                    sv.inc(mm)
                                block.vector(_)
                                sst_marks[hg] = sv.n
                                psc_free[0] = sv.n

                                if hg == 0 and pend_wo[0] is not None:
                                    emit_wo_stage(*pend_wo[0])
                                    pend_wo[0] = None

                            if a2sub < 4:
                                continue

                            for hg in range(2):
                                def _(te, t=t, trel=trel, hg=hg, m_sstv=sst_marks[hg],
                                      paf=pattn_free[0], fm=fold_mark[0]):
                                    te.wait_ge(sv.h, m_sstv)
                                    te.wait_ge(sv.h, paf)
                                    te.wait_ge(sv.h, fm)
                                    for hrel in range(8):
                                        h = hg * 8 + hrel
                                        rb = (h % 2) * 64
                                        ec = h // 2
                                        ecrel = ec - hg * 4
                                        pos = (hrel % 2) * 512 + (hrel // 2) * 128
                                        te.matmul(
                                            pattn[rb:rb + 64, ecrel * 128:(ecrel + 1) * 128],
                                            vtm[:, t * D + h * 64:t * D + (h + 1) * 64],
                                            sstb[hg][:, pos:pos + 128],
                                            start=True, stop=False)
                                        mm = te.matmul(
                                            pattn[rb:rb + 64, ecrel * 128:(ecrel + 1) * 128],
                                            snap[rb:rb + 64, t * 512 + ec * 64:
                                                 t * 512 + ec * 64 + 64],
                                            qfm[rb:rb + 64, ec * 512 + trel * 128:
                                                ec * 512 + (trel + 1) * 128],
                                            start=False, stop=True)
                                    st.inc(mm)
                                block.tensor(_)
                                m_at = st.n

                                def _(v, hg=hg, m_at=m_at):
                                    v.wait_ge(st.h, m_at)
                                    sv.inc(v.tensor_copy(sattn[:, hg * 512:(hg + 1) * 512],
                                                         pattn[:, 0:512]))
                                block.vector(_)
                                attn_marks.append(sv.n)
                                pattn_free[0] = sv.n

                            if a2sub < 5:
                                continue
                            pend_wo[0] = (t, trel, attn_marks[-1])

                    if pend_wo[0] is not None:
                        emit_wo_stage(*pend_wo[0])
                        pend_wo[0] = None

            if debug and stage >= 3:
                with nc.Block() as block:
                    def _(sy):
                        sy.wait_ge(sv.h, sv.n)
                        sd_o.inc(sy.dma_start(
                            out=dbg["xattn"].rearrange("(t p) f -> p t f", p=128),
                            in_=x_res[:].rearrange("p (t f) -> p t f", t=NT)), 16)
                    block.sync(_)

        if stage >= 4:
          with ExitStack() as fscope:
            ef = fscope.enter_context
            xlnfF = ef(nc.sbuf_tensor("xlnfF", [128, 8 * 1024], BF16))
            gwork = ef(nc.sbuf_tensor("gwork", [128, 384], F32))
            hid = ef(nc.sbuf_tensor("hid", [128, 8 * 1024], BF16))
            hsh = ef(nc.sbuf_tensor("hsh", [128, 8 * 1024], BF16))
            accum = ef(nc.sbuf_tensor("accum", [128, 8 * 1024], F32))
            wdense = ef(nc.sbuf_tensor("wdense", [128, 128], F32))
            wpair = ef(nc.sbuf_tensor("wpair", [128, 2 * 2048], BF16))
            wout_s = ef(nc.sbuf_tensor("wout_s", [128, 8 * 1024], BF16))
            w2A = ef(nc.sbuf_tensor("w2A", [128, 8 * 1024], BF16))
            ssil = ef(nc.sbuf_tensor("ssil", [128, 1024], F32))

            fmarks = {}
            hid_mark = 0
            f1_state = {}

            def make_f1ab_steps(hf, base0):
                T0 = hf * 8
                WD = hf * 64
                steps = []

                def f_srcs(hg):
                    return [(x_res[:, t * D:t * D + 512],
                             x_res[:, t * D + 512:(t + 1) * D])
                            for t in range(T0 + hg * 4, T0 + (hg + 1) * 4)]

                for half_grp in range(2):
                    def chain_step(block, ptr, pg, hg=half_grp):
                        f1_state[(hf, "chain", hg)] = emit_ln_chain(
                            block, f_srcs(hg), base0 + hg * 4)
                    steps.append(chain_step)

                    for k in range(QT):
                        def apply_step(block, ptr, pg, hg=half_grp, k=k):
                            trel = hg * 4 + k
                            t = T0 + trel
                            slot = trel % 2
                            m_ap = emit_apply(block, base0 + trel,
                                              x_res[:, t * D:(t + 1) * D],
                                              sapply[:, slot * D:(slot + 1) * D],
                                              sap_war[slot], f1_state[(hf, "chain", hg)])
                            f1_state[(hf, "ap", trel)] = m_ap
                            f1_state[(hf, "apslot", trel)] = slot
                        steps.append(apply_step)

                        def tr_step(block, ptr, pg, hg=half_grp, k=k):
                            trel = hg * 4 + k
                            slot = f1_state[(hf, "apslot", trel)]
                            pf = f1_state.setdefault((hf, "ptr_free"), [sv.n])
                            dstv = xlnfF[:].rearrange("p (c w) -> p c w", c=8)[
                                :, :, trel * 128:(trel + 1) * 128]
                            m_tr, m_xl = emit_transpose(block, ptr, pf, slot, dstv,
                                                        f1_state[(hf, "ap", trel)])
                            sap_war[slot] = ("t", m_tr)
                            f1_state[(hf, "xl", trel)] = m_xl
                        steps.append(tr_step)

                        def gate_step(block, ptr, pg, hg=half_grp, k=k):
                            trel = hg * 4 + k
                            pgf = f1_state.setdefault((hf, "pg_free"), [sv.n])

                            def _(te, trel=trel, m_xl=f1_state[(hf, "xl", trel)],
                                  pgfv=pgf[0]):
                                te.wait_ge(sv.h, m_xl)
                                te.wait_ge(sv.h, pgfv)
                                te.wait_ge(s_row.h, row_mark)
                                for cc in range(8):
                                    te.matmul(pg[:, 0:8],
                                              xlnfF[:, cc * 1024 + trel * 128:
                                                   cc * 1024 + (trel + 1) * 128],
                                              wgate_b[:, cc * 8:(cc + 1) * 8],
                                              start=(cc == 0), stop=False)
                                mm = te.matmul(pg[:, 0:8], onesr[:], gbr[0:1, 0:8],
                                               start=False, stop=True)
                                st.inc(mm)
                            block.tensor(_)
                            m_g = st.n

                            def _(v, trel=trel, m_g=m_g):
                                v.wait_ge(st.h, m_g)
                                v.wait_ge(sv.h, sv.n)
                                sv.inc(v.tensor_copy(
                                    gwork[:, LG + trel * 8:LG + (trel + 1) * 8],
                                    pg[:, 0:8]))
                            block.vector(_)
                            pgf[0] = sv.n
                        steps.append(gate_step)

                def softmax_step(block, ptr, pg):
                    def _(v):
                        def step(inst):
                            sv.inc(inst)
                            v.wait_ge(sv.h, sv.n)
                        v.wait_ge(sv.h, sv.n)
                        mm = None
                        for ti in range(8):
                            mm = v.reduce_max(gwork[:, GMX + ti:GMX + ti + 1],
                                              gwork[:, LG + ti * 8:LG + (ti + 1) * 8],
                                              axis=AX.X)
                        step(mm)
                        for ti in range(8):
                            mm = v.tensor_scalar(gwork[:, EB + ti * 8:EB + (ti + 1) * 8],
                                                 gwork[:, LG + ti * 8:LG + (ti + 1) * 8],
                                                 gwork[:, GMX + ti:GMX + ti + 1], None,
                                                 OP.subtract)
                        step(mm)
                    block.vector(_)
                    m_pre = sv.n

                    def _(s, m_pre=m_pre):
                        s.wait_ge(sv.h, m_pre)
                        sa.inc(s.activation(gwork[:, PB:PB + 64], gwork[:, EB:EB + 64],
                                            AF.Exp))
                    block.scalar(_)
                    m_exp = sa.n

                    def _(v, m_exp=m_exp):
                        def step(inst):
                            sv.inc(inst)
                            v.wait_ge(sv.h, sv.n)
                        v.wait_ge(sa.h, m_exp)
                        v.wait_ge(sv.h, sv.n)
                        mm = None
                        for ti in range(8):
                            mm = v.reduce_sum(gwork[:, GSUM + ti:GSUM + ti + 1],
                                              gwork[:, PB + ti * 8:PB + (ti + 1) * 8],
                                              axis=AX.X)
                        step(mm)
                        step(v.reciprocal(gwork[:, GRS:GRS + 8], gwork[:, GSUM:GSUM + 8]))
                        for ti in range(8):
                            mm = v.tensor_scalar(gwork[:, PB + ti * 8:PB + (ti + 1) * 8],
                                                 gwork[:, PB + ti * 8:PB + (ti + 1) * 8],
                                                 gwork[:, GRS + ti:GRS + ti + 1], None,
                                                 OP.mult)
                        step(mm)
                        for ti in range(8):
                            mm = v.reduce_max(gwork[:, GM1 + ti:GM1 + ti + 1],
                                              gwork[:, PB + ti * 8:PB + (ti + 1) * 8],
                                              axis=AX.X)
                        step(mm)
                        for ti in range(8):
                            mm = v.tensor_scalar(gwork[:, GE + ti * 8:GE + (ti + 1) * 8],
                                                 gwork[:, PB + ti * 8:PB + (ti + 1) * 8],
                                                 gwork[:, GM1 + ti:GM1 + ti + 1], None,
                                                 OP.is_ge)
                        step(mm)
                        for ti in range(8):
                            mm = v.scalar_tensor_tensor(
                                gwork[:, MB + ti * 8:MB + (ti + 1) * 8],
                                gwork[:, GE + ti * 8:GE + (ti + 1) * 8], NEG,
                                gwork[:, PB + ti * 8:PB + (ti + 1) * 8],
                                OP.mult, OP.add)
                        step(mm)
                        for ti in range(8):
                            mm = v.reduce_max(gwork[:, GM2 + ti:GM2 + ti + 1],
                                              gwork[:, MB + ti * 8:MB + (ti + 1) * 8],
                                              axis=AX.X)
                        step(mm)
                        for ti in range(8):
                            mm = v.tensor_scalar(gwork[:, GE + ti * 8:GE + (ti + 1) * 8],
                                                 gwork[:, PB + ti * 8:PB + (ti + 1) * 8],
                                                 gwork[:, GM2 + ti:GM2 + ti + 1], None,
                                                 OP.is_ge)
                        step(mm)
                        for ti in range(8):
                            mm = v.tensor_tensor(wdense[:, WD + ti * 8:WD + (ti + 1) * 8],
                                                 gwork[:, PB + ti * 8:PB + (ti + 1) * 8],
                                                 gwork[:, GE + ti * 8:GE + (ti + 1) * 8],
                                                 OP.mult)
                        step(mm)
                    block.vector(_)
                steps.append(softmax_step)
                return steps

            f1ab_steps = {0: make_f1ab_steps(0, 0), 1: make_f1ab_steps(1, 8)}

            for hf in range(2):
                if hf == 1 and stage < 6:
                    continue
                T0 = hf * 8
                with ExitStack() as ps:
                    ptr = ps.enter_context(nc.psum_tensor(f"ptrF{hf}", [128, 1024], BF16))
                    pg = ps.enter_context(nc.psum_tensor(f"pg{hf}", [128, 8], F32))
                    pua0 = ps.enter_context(nc.psum_tensor(f"pua0{hf}", [128, 512], F32))
                    pub0 = ps.enter_context(nc.psum_tensor(f"pub0{hf}", [128, 512], F32))
                    pua1 = ps.enter_context(nc.psum_tensor(f"pua1{hf}", [128, 512], F32))
                    pub1 = ps.enter_context(nc.psum_tensor(f"pub1{hf}", [128, 512], F32))
                    puas = [pua0, pua1]
                    pubs = [pub0, pub1]
                    psh = ps.enter_context(nc.psum_tensor(f"psh{hf}", [128, 1024], F32))

                    def emit_pair(gp, wkey, hf, pp, src_d):
                        swp = s_wp0 if pp % 2 == 0 else s_wp1
                        fmarks[(wkey, hf, pp)] = (swp, swp.inc(gp.dma_start(
                            out=wpair[:, (pp % 2) * 2048:(pp % 2) * 2048 + 2048]
                                .rearrange("p (dc e) -> p dc e", dc=8),
                            in_=src_d[:, pp * 256:(pp + 1) * 256]
                                .rearrange("(dc p) e -> p dc e", p=128)), 16))

                    with nc.Block() as block:
                        def _(gp, hf=hf):
                            emit_pair(gp, "w1", hf, 0, w1p_d)
                            emit_pair(gp, "w1", hf, 1, w1p_d)
                        block.gpsimd(_)

                        for stp in f1ab_steps[hf]:
                            stp(block, ptr, pg)

                        pua_free = {0: sa.n, 1: sa.n}
                        pub_free = {0: sv.n, 1: sv.n}
                        ub_ix = [0]
                        for dst, wkey, wsrc_d in ((hid, "w1", w1p_d), (hsh, "win", winp_d)):
                            if wkey == "win":
                                def _(gp, hf=hf, m=fmarks[("w1mm", hf)]):
                                    gp.wait_ge(st.h, m)
                                    emit_pair(gp, "win", hf, 0, winp_d)
                                    emit_pair(gp, "win", hf, 1, winp_d)
                                    fmarks[("wout", hf)] = (s_wB, s_wB.inc(gp.dma_start(
                                        out=wout_s[:].rearrange("p (dc e) -> p dc e", dc=8),
                                        in_=wout_d.rearrange("(dc p) e -> p dc e", p=128)),
                                        16))
                                    fmarks[("w2", hf, 0)] = (s_wA, s_wA.inc(gp.dma_start(
                                        out=w2A[:].rearrange("p (dc e) -> p dc e", dc=8),
                                        in_=w2_d[0 * D:(0 + 1) * D, :]
                                            .rearrange("(dc p) e -> p dc e", p=128)), 16))
                                block.gpsimd(_)

                            for pp in range(8):
                                wslice = wpair[:, (pp % 2) * 2048:(pp % 2) * 2048 + 2048]
                                for tg in range(2):
                                    ub = ub_ix[0]
                                    ub_ix[0] = 1 - ub
                                    def _(te, pp=pp, tg=tg, wkey=wkey, hf=hf, wslice=wslice,
                                          ub=ub, puaf=pua_free[ub], pubf=pub_free[ub]):
                                        wp_s, wp_m = fmarks[(wkey, hf, pp)]
                                        te.wait_ge(wp_s.h, wp_m)
                                        te.wait_ge(sa.h, puaf)
                                        te.wait_ge(sv.h, pubf)
                                        for cc in range(8):
                                            mma = te.matmul(
                                                puas[ub][:, 0:512],
                                                wslice[:, cc * 256:cc * 256 + 128],
                                                xlnfF[:, cc * 1024 + tg * 512:
                                                      cc * 1024 + (tg + 1) * 512],
                                                start=(cc == 0), stop=(cc == 7))
                                        st.inc(mma)
                                        for cc in range(8):
                                            mmb = te.matmul(
                                                pubs[ub][:, 0:512],
                                                wslice[:, cc * 256 + 128:cc * 256 + 256],
                                                xlnfF[:, cc * 1024 + tg * 512:
                                                      cc * 1024 + (tg + 1) * 512],
                                                start=(cc == 0), stop=(cc == 7))
                                        st.inc(mmb)
                                    block.tensor(_)
                                    m_a_mm = st.n - 1
                                    m_b_mm = st.n

                                    if tg == 1 and pp + 2 < 8:
                                        def _(gp, wkey=wkey, hf=hf, pp=pp, m_b_mm=m_b_mm,
                                              src_d=wsrc_d):
                                            gp.wait_ge(st.h, m_b_mm)
                                            emit_pair(gp, wkey, hf, pp + 2, src_d)
                                        block.gpsimd(_)

                                    def _(s, ub=ub, m_a_mm=m_a_mm, pubf=pub_free[ub]):
                                        s.wait_ge(st.h, m_a_mm)
                                        s.wait_ge(sv.h, pubf)
                                        sa.inc(s.activation(ssil[:, ub * 512:(ub + 1) * 512],
                                                            puas[ub][:, 0:512], AF.Silu))
                                    block.scalar(_)
                                    pua_free[ub] = sa.n
                                    m_sil = sa.n

                                    def _(v, dst=dst, pp=pp, tg=tg, ub=ub, m_b_mm=m_b_mm,
                                          m_sil=m_sil):
                                        v.wait_ge(st.h, m_b_mm)
                                        v.wait_ge(sa.h, m_sil)
                                        sv.inc(v.tensor_tensor(
                                            dst[:, pp * 1024 + tg * 512:
                                                pp * 1024 + (tg + 1) * 512],
                                            pubs[ub][:, 0:512], ssil[:, ub * 512:(ub + 1) * 512],
                                            OP.mult))
                                    block.vector(_)
                                    pub_free[ub] = sv.n
                            if wkey == "w1":
                                fmarks[("w1mm", hf)] = st.n
                        hid_mark = sv.n

                        psh_free = [sv.n]
                        for trel in range(8):
                            def _(te, trel=trel, hf=hf, hid_mark=hid_mark, pshf=psh_free[0]):
                                wo_s, wo_m = fmarks[("wout", hf)]
                                te.wait_ge(sv.h, hid_mark)
                                te.wait_ge(wo_s.h, wo_m)
                                te.wait_ge(sv.h, pshf)
                                for j in range(2):
                                    for jp in range(8):
                                        mm = te.matmul(
                                            psh[:, j * 512:(j + 1) * 512],
                                            hsh[:, jp * 1024 + trel * 128:
                                                jp * 1024 + (trel + 1) * 128],
                                            wout_s[:, jp * 1024 + j * 512:
                                                   jp * 1024 + (j + 1) * 512],
                                            start=(jp == 0), stop=(jp == 7))
                                st.inc(mm)
                            block.tensor(_)
                            m_s = st.n

                            def _(v, trel=trel, m_s=m_s):
                                v.wait_ge(st.h, m_s)
                                v.wait_ge(sv.h, sv.n)
                                sv.inc(v.tensor_copy(accum[:, trel * 1024:(trel + 1) * 1024],
                                                     psh[:, 0:1024]))
                            block.vector(_)
                            psh_free[0] = sv.n
                        fmarks[("fs_done", hf)] = st.n

                if stage < 5:
                    continue
                with ExitStack() as ps:
                    pr0 = ps.enter_context(nc.psum_tensor(f"pr0_{hf}", [128, 1024], F32))
                    pr1 = ps.enter_context(nc.psum_tensor(f"pr1_{hf}", [128, 1024], F32))
                    prs = [pr0, pr1]
                    do_f1ab_h1 = False
                    if do_f1ab_h1:
                        ptrF2 = ps.enter_context(nc.psum_tensor("ptrF2", [128, 1024], BF16))
                        pgF2 = ps.enter_context(nc.psum_tensor("pgF2", [128, 8], F32))

                    with nc.Block() as block:
                        n_cons = [0]

                        def consume_h1(upto):
                            while do_f1ab_h1 and n_cons[0] < upto:
                                f1ab_steps[1][n_cons[0]](block, ptrF2, pgF2)
                                n_cons[0] += 1

                        last_mm_of_expert = {}
                        for e in range(8):
                            slot = w2A if (e % 2 == 0) else wout_s
                            if e != 0:
                                def _(gp, e=e, hf=hf, slot=slot):
                                    if e >= 2:
                                        gp.wait_ge(st.h, last_mm_of_expert[e - 2])
                                    sw2 = s_wA if e % 2 == 0 else s_wB
                                    fmarks[("w2", hf, e)] = (sw2, sw2.inc(gp.dma_start(
                                        out=slot[:].rearrange("p (dc e2) -> p dc e2", dc=8),
                                        in_=w2_d[e * D:(e + 1) * D, :]
                                            .rearrange("(dc p) e2 -> p dc e2", p=128)), 16))
                                block.gpsimd(_)

                            pr_free = {0: sv.n, 1: sv.n}
                            for trel in range(8):
                                pb = trel % 2
                                def _(te, e=e, trel=trel, pb=pb, hf=hf, slot=slot,
                                      prf=pr_free[pb], hid_mark=hid_mark):
                                    w2_s, w2_m = fmarks[("w2", hf, e)]
                                    te.wait_ge(w2_s.h, w2_m)
                                    te.wait_ge(sv.h, prf)
                                    te.wait_ge(sv.h, hid_mark)
                                    for j in range(2):
                                        for jp in range(8):
                                            mm = te.matmul(
                                                prs[pb][:, j * 512:(j + 1) * 512],
                                                hid[:, jp * 1024 + trel * 128:
                                                    jp * 1024 + (trel + 1) * 128],
                                                slot[:, jp * 1024 + j * 512:
                                                     jp * 1024 + (j + 1) * 512],
                                                start=(jp == 0), stop=(jp == 7))
                                    st.inc(mm)
                                block.tensor(_)
                                m_r = st.n

                                def _(v, e=e, trel=trel, pb=pb, m_r=m_r):
                                    v.wait_ge(st.h, m_r)
                                    v.wait_ge(sv.h, sv.n)
                                    sv.inc(v.scalar_tensor_tensor(
                                        accum[:, trel * 1024:(trel + 1) * 1024],
                                        prs[pb][:, 0:1024],
                                        wdense[:, hf * 64 + trel * 8 + e:
                                                hf * 64 + trel * 8 + e + 1],
                                        accum[:, trel * 1024:(trel + 1) * 1024],
                                        OP.mult, OP.add))
                                block.vector(_)
                                pr_free[pb] = sv.n
                                if e == 7 and trel == 3:
                                    ln4c = [emit_ln_chain(
                                        block,
                                        [(accum[:, r * 1024:r * 1024 + 512],
                                          accum[:, r * 1024 + 512:(r + 1) * 1024])
                                         for r in range(4)], 0)]
                            last_mm_of_expert[e] = st.n
                            consume_h1((e + 1) * 27 // 8)

                        consume_h1(27)

                        res_marks = []
                        for half_grp in range(2):
                            if half_grp == 0:
                                m_chain = ln4c[0]
                            else:
                                srcs4 = [(accum[:, trel * 1024:trel * 1024 + 512],
                                          accum[:, trel * 1024 + 512:(trel + 1) * 1024])
                                         for trel in range(4, 8)]
                                m_chain = emit_ln_chain(block, srcs4, 4)
                            for k in range(QT):
                                trel = half_grp * 4 + k
                                t = T0 + trel
                                slot = trel % 2
                                m4 = emit_apply(block, trel,
                                                accum[:, trel * 1024:(trel + 1) * 1024],
                                                sapply[:, slot * D:(slot + 1) * D],
                                                sap_war[slot], m_chain)

                                def _(v, t=t, slot=slot):
                                    v.wait_ge(sv.h, sv.n)
                                    sv.inc(v.tensor_tensor(
                                        x_res[:, t * D:(t + 1) * D],
                                        x_res[:, t * D:(t + 1) * D],
                                        sapply[:, slot * D:(slot + 1) * D], OP.add))
                                block.vector(_)
                                sap_war[slot] = ("v", sv.n)
                                res_marks.append(sv.n)

                        def _(sy, hf=hf, m=res_marks[-1]):
                            sy.wait_ge(sv.h, m)
                            sd_o.inc(sy.dma_start(
                                out=y_d.rearrange("(t p) f -> p t f", p=128)[
                                    :, hf * 8:(hf + 1) * 8, :],
                                in_=x_res[:, hf * 8 * D:(hf + 1) * 8 * D]
                                    .rearrange("p (t f) -> p t f", t=8)), 16)
                        block.sync(_)

            if debug:
                with nc.Block() as block:
                    def _(sy):
                        sy.wait_ge(sv.h, sv.n)
                        sd_o.inc(sy.dma_start(out=dbg["wdense"], in_=wdense[:]), 16)
                        sd_o.inc(sy.dma_start(out=dbg["hid"], in_=hid[:]), 16)
                        sd_o.inc(sy.dma_start(out=dbg["hsh"], in_=hsh[:]), 16)
                        sd_o.inc(sy.dma_start(out=dbg["accum"], in_=accum[:]), 16)
                        sy.wait_ge(sd_o.h, sd_o.n)
                    block.sync(_)
            else:
                with nc.Block() as block:
                    def _(sy):
                        sy.wait_ge(sd_o.h, sd_o.n)
                    block.sync(_)
        else:
            with nc.Block() as block:
                def _(sy):
                    sy.wait_ge(sd_o.h, sd_o.n)
                block.sync(_)

    return nc


def make_inputs(inputs):
    x = np.ascontiguousarray(inputs["L0_attn_x"], dtype=np.float32)
    wqT = np.ascontiguousarray(inputs["L0_attn_wq_w"].T)
    wkT = np.ascontiguousarray(inputs["L0_attn_wk_w"].T)
    def packch(w):
        return np.ascontiguousarray(
            w.reshape(8, 128, 8, 128).transpose(2, 1, 0, 3).reshape(1024, 1024))
    wkp = packch(wkT)
    wqp = packch(wqT)
    wvT = np.ascontiguousarray(inputs["L0_attn_wv_w"].T)
    woT = np.ascontiguousarray(inputs["L0_attn_wo_w"].T)
    perm = np.concatenate([np.r_[p * 128:(p + 1) * 128, (p + 8) * 128:(p + 9) * 128]
                           for p in range(8)])
    w1p = np.ascontiguousarray(inputs["L0_ffn_w1_shared"][:, perm])
    winp = np.ascontiguousarray(inputs["L0_ffn_w_shared_in"][:, perm])
    wout = np.ascontiguousarray(inputs["L0_ffn_w_shared_out"])
    wgate = np.ascontiguousarray(inputs["L0_ffn_w_gate"])
    w2 = np.ascontiguousarray(inputs["L0_ffn_w2"].reshape(8 * 1024, 1024))
    bqc = np.ascontiguousarray(inputs["L0_attn_wq_b"].reshape(8, 128).T)
    bkc = np.ascontiguousarray(inputs["L0_attn_wk_b"].reshape(8, 128).T)
    bvr = np.ascontiguousarray(inputs["L0_attn_wv_b"].reshape(1, 1024))
    bor = np.ascontiguousarray(inputs["L0_attn_wo_b"].reshape(1, 1024))
    gbr = np.ascontiguousarray(inputs["L0_ffn_gate_bias"].reshape(1, 8))
    ident = np.eye(128, dtype=BF)
    maskT = np.triu(np.ones((128, 128), np.float32)).astype(BF)
    onesr = np.ones((1, 128), dtype=BF)
    magici = np.full((128, 8), MAGIC, np.int32)

    in_maps = []
    for c in range(8):
        b, hf = c // 2, c % 2
        sel = np.zeros((128, 8), np.float32)
        sel[:, 0] = float(hf)
        in_maps.append(dict(
            x=np.ascontiguousarray(x[b, hf * 2048:(hf + 1) * 2048]),
            wqT=wqT, wkT=wkT, wkp=wkp, wqp=wqp, wvT=wvT, woT=woT,
            w1p=w1p, winp=winp, wout=wout, wgate=wgate, w2=w2,
            bqc=bqc, bkc=bkc, bvr=bvr, bor=bor, gbr=gbr,
            ident=ident, maskT=maskT, onesr=onesr, sel=sel, magici=magici,
        ))
    return in_maps


def kernel(**inputs):
    in_maps = make_inputs(inputs)
    nc = build(debug=False)
    res = run_bass_kernel_spmd(nc, in_maps, core_ids=list(range(8)))
    out = np.empty((4, 4096, 1024), np.float32)
    for c in range(8):
        b, hf = c // 2, c % 2
        out[b, hf * 2048:(hf + 1) * 2048] = res.results[c]["y"]
    return out


# version 16
# speedup vs baseline: 1.1986x; 1.1967x over previous
import numpy as np
import ml_dtypes
import concourse.bass as bass
import concourse.mybir as mybir
from concourse.alu_op_type import AluOpType as OP
from concourse.bass_utils import run_bass_kernel_spmd
from contextlib import ExitStack

F32 = mybir.dt.float32
BF16 = mybir.dt.bfloat16
I32 = mybir.dt.int32
AF = mybir.ActivationFunctionType
AX = mybir.AxisListType
BF = ml_dtypes.bfloat16

D = 1024
NH = 16
TOK = 2048
NT = 16
QT = 4
EPS = 1e-5
NEG = -1e30
MAGIC = 0x5F3759DF


class Sem:
    def __init__(self, ctx, nc, name):
        self.h = ctx.enter_context(nc.semaphore(name))
        self.n = 0

    def inc(self, inst, k=1):
        inst.then_inc(self.h, k)
        self.n += k
        return self.n


def build(debug=False, stage=99, a2sub=6):
    nc = bass.Bass("TRN2", target_bir_lowering=False, debug=False)

    dt = lambda name, shape, dty=F32: nc.dram_tensor(name, shape, dty, kind="ExternalInput").ap()
    x_d = dt("x", [TOK, D])
    wqT_d = dt("wqT", [D, D])
    wkT_d = dt("wkT", [D, D])
    wkp_d = dt("wkp", [D, D])
    wqp_d = dt("wqp", [D, D])
    wvT_d = dt("wvT", [D, D])
    woT_d = dt("woT", [D, D])
    w1p_d = dt("w1p", [D, 2 * D])
    winp_d = dt("winp", [D, 2 * D])
    wout_d = dt("wout", [D, D])
    wgate_d = dt("wgate", [D, 8])
    w2_d = dt("w2", [8 * D, D])
    bq_d = dt("bqc", [128, 8])
    bk_d = dt("bkc", [128, 8])
    bv_d = dt("bvr", [1, D])
    bo_d = dt("bor", [1, D])
    gb_d = dt("gbr", [1, 8])
    ident_d = nc.dram_tensor("ident", [128, 128], BF16, kind="ExternalInput").ap()
    mask_d = nc.dram_tensor("maskT", [128, 128], BF16, kind="ExternalInput").ap()
    ones_d = nc.dram_tensor("onesr", [1, 128], BF16, kind="ExternalInput").ap()
    sel_d = dt("sel", [128, 8])
    magic_d = nc.dram_tensor("magici", [128, 8], I32, kind="ExternalInput").ap()

    y_d = nc.dram_tensor("y", [TOK, D], F32, kind="ExternalOutput").ap()
    dbg = {}
    if debug:
        dbg["srun"] = nc.dram_tensor("dbg_srun", [128, 512], F32, kind="ExternalOutput").ap()
        dbg["sbase"] = nc.dram_tensor("dbg_sbase", [128, 512], F32, kind="ExternalOutput").ap()
        dbg["xattn"] = nc.dram_tensor("dbg_xattn", [TOK, D], F32, kind="ExternalOutput").ap()
        dbg["wdense"] = nc.dram_tensor("dbg_wdense", [128, 128], F32, kind="ExternalOutput").ap()
        dbg["hid"] = nc.dram_tensor("dbg_hid", [128, 8 * 1024], BF16, kind="ExternalOutput").ap()
        dbg["hsh"] = nc.dram_tensor("dbg_hsh", [128, 8 * 1024], BF16, kind="ExternalOutput").ap()
        dbg["accum"] = nc.dram_tensor("dbg_accum", [128, 8 * 1024], F32, kind="ExternalOutput").ap()
        dbg["xres"] = nc.dram_tensor("dbg_xres", [TOK, D], F32, kind="ExternalOutput").ap()

    cc_in = nc.dram_tensor("cc_in", [128, 512], F32)
    cc_out = nc.dram_tensor("cc_out", [256, 512], F32)

    with ExitStack() as g:
        eg = g.enter_context
        x_res = eg(nc.sbuf_tensor("x_res", [128, NT * D], F32))
        sapply = eg(nc.sbuf_tensor("sapply", [128, 2 * D], BF16))
        statsr = eg(nc.sbuf_tensor("statsr", [128, 192], F32))
        lnmv = eg(nc.sbuf_tensor("lnmv", [128, 32], F32))
        lnu = eg(nc.sbuf_tensor("lnu", [128, 16], F32))
        lnh = eg(nc.sbuf_tensor("lnh", [128, 16], F32))
        lnz = eg(nc.sbuf_tensor("lnz", [128, 16], F32))
        lnt = eg(nc.sbuf_tensor("lnt", [128, 16], F32))
        lnb = eg(nc.sbuf_tensor("lnb", [128, 16], F32))
        st2 = eg(nc.sbuf_tensor("st2", [128, 24], F32))
        smag = eg(nc.sbuf_tensor("smag", [128, 8], I32))
        ident = eg(nc.sbuf_tensor("ident_s", [128, 128], BF16))
        onesr = eg(nc.sbuf_tensor("onesr_s", [1, 128], BF16))
        bvr = eg(nc.sbuf_tensor("bvr_s", [1, D], BF16))
        bor = eg(nc.sbuf_tensor("bor_s", [1, D], BF16))
        gbr = eg(nc.sbuf_tensor("gbr_s", [1, 8], BF16))
        bqc = eg(nc.sbuf_tensor("bqc_s", [128, 8], F32))
        bkc = eg(nc.sbuf_tensor("bkc_s", [128, 8], F32))
        wgate_b = eg(nc.sbuf_tensor("wgate_b", [128, 64], BF16))
        selb = eg(nc.sbuf_tensor("sel_s", [128, 8], F32))

        sd_x = Sem(g, nc, "sd_x")
        sd_x1 = Sem(g, nc, "sd_x1")
        sd_x2 = Sem(g, nc, "sd_x2")
        sd_x3 = Sem(g, nc, "sd_x3")
        sd_w = Sem(g, nc, "sd_w")
        s_wv = Sem(g, nc, "s_wv")
        s_row = Sem(g, nc, "s_row")
        s_ck0 = Sem(g, nc, "s_ck0")
        s_ck1 = Sem(g, nc, "s_ck1")
        s_wp0 = Sem(g, nc, "s_wp0")
        s_wp1 = Sem(g, nc, "s_wp1")
        s_wA = Sem(g, nc, "s_wA")
        s_wB = Sem(g, nc, "s_wB")
        sd_cc = Sem(g, nc, "sd_cc")
        sd_o = Sem(g, nc, "sd_o")
        st = Sem(g, nc, "st")
        sv = Sem(g, nc, "sv")
        sa = Sem(g, nc, "sa")
        sg = Sem(g, nc, "sg")
        sp = Sem(g, nc, "sp")

        LG, EB, PB, GE, MB = 0, 64, 128, 192, 256
        GMX, GSUM, GRS, GM1, GM2 = 320, 328, 336, 344, 352

        sap_war = [("t", 0), ("t", 0)]

        def wait_war(eng, war):
            kind, mark = war
            if mark:
                eng.wait_ge(st.h if kind == "t" else sv.h, mark)

        def emit_ln_chain(block, srcs, base, extra_wait_t=None, extra_wait_a=None):
            n = len(srcs)
            b = base

            def _(v):
                if extra_wait_t is not None:
                    v.wait_ge(st.h, extra_wait_t)
                if extra_wait_a is not None:
                    v.wait_ge(sa.h, extra_wait_a)
                v.wait_ge(sv.h, sv.n)

                def step(inst):
                    sv.inc(inst)
                    v.wait_ge(sv.h, sv.n)

                mm = None
                for i, (p1, p2) in enumerate(srcs):
                    v.bn_stats(statsr[:, (b + i) * 12:(b + i) * 12 + 6], p1)
                    mm = v.bn_stats(statsr[:, (b + i) * 12 + 6:(b + i) * 12 + 12], p2)
                step(mm)
                for i in range(n):
                    mm = v.bn_aggr(lnmv[:, (b + i) * 2:(b + i + 1) * 2],
                                   statsr[:, (b + i) * 12:(b + i + 1) * 12])
                step(mm)
                mv3 = lnmv[:].rearrange("p (i two) -> p two i", two=2)
                var_v = mv3[:, 1:2, b:b + n]
                mean_v = mv3[:, 0:1, b:b + n]
                step(v.tensor_scalar(lnu[:, b:b + n], var_v, EPS, None, OP.add))
                step(v.tensor_scalar(lnz[:, b:b + n].bitcast(I32),
                                     lnu[:, b:b + n].bitcast(I32), 1,
                                     None, OP.arith_shift_right))
                step(v.tensor_tensor(lnz[:, b:b + n].bitcast(I32), smag[:, 0:n],
                                     lnz[:, b:b + n].bitcast(I32), OP.subtract))
                step(v.tensor_scalar(lnh[:, b:b + n], lnu[:, b:b + n], 0.5, None, OP.mult))
                for _i in range(2):
                    step(v.tensor_tensor(lnt[:, b:b + n], lnz[:, b:b + n], lnz[:, b:b + n],
                                         OP.mult))
                    step(v.tensor_tensor(lnt[:, b:b + n], lnt[:, b:b + n], lnh[:, b:b + n],
                                         OP.mult))
                    step(v.tensor_scalar(lnt[:, b:b + n], lnt[:, b:b + n], -1.0, 1.5,
                                         OP.mult, OP.add))
                    step(v.tensor_tensor(lnz[:, b:b + n], lnz[:, b:b + n], lnt[:, b:b + n],
                                         OP.mult))
                step(v.scalar_tensor_tensor(lnb[:, b:b + n], mean_v, -1.0, lnz[:, b:b + n],
                                            OP.mult, OP.mult))
            block.vector(_)
            return sv.n

        def emit_apply(block, i, asrc, dst, war, m_chain):
            def _(v, i=i, asrc=asrc, dst=dst, war=war):
                wait_war(v, war)
                v.wait_ge(sv.h, sv.n)
                sv.inc(v.tensor_scalar(dst, asrc, lnz[:, i:i + 1], lnb[:, i:i + 1],
                                       OP.mult, OP.add))
            block.vector(_)
            return sv.n

        def emit_ln2(block, p1, p2, asrc, dst, war, extra_wait_t):
            def _(v):
                v.wait_ge(st.h, extra_wait_t)
                v.wait_ge(sv.h, sv.n)

                def step(inst):
                    sv.inc(inst)
                    v.wait_ge(sv.h, sv.n)

                v.bn_stats(st2[:, 0:6], p1)
                step(v.bn_stats(st2[:, 6:12], p2))
                step(v.bn_aggr(st2[:, 12:14], st2[:, 0:12]))
                step(v.tensor_scalar(st2[:, 14:15], st2[:, 13:14], EPS, None, OP.add))
                step(v.tensor_scalar(st2[:, 16:17].bitcast(I32), st2[:, 14:15].bitcast(I32),
                                     1, None, OP.arith_shift_right))
                step(v.tensor_tensor(st2[:, 16:17].bitcast(I32), smag[:, 0:1],
                                     st2[:, 16:17].bitcast(I32), OP.subtract))
                step(v.tensor_scalar(st2[:, 15:16], st2[:, 14:15], 0.5, None, OP.mult))
                for _i in range(2):
                    step(v.tensor_tensor(st2[:, 17:18], st2[:, 16:17], st2[:, 16:17], OP.mult))
                    step(v.tensor_tensor(st2[:, 17:18], st2[:, 17:18], st2[:, 15:16], OP.mult))
                    step(v.tensor_scalar(st2[:, 17:18], st2[:, 17:18], -1.0, 1.5,
                                         OP.mult, OP.add))
                    step(v.tensor_tensor(st2[:, 16:17], st2[:, 16:17], st2[:, 17:18], OP.mult))
                step(v.scalar_tensor_tensor(st2[:, 18:19], st2[:, 12:13], -1.0, st2[:, 16:17],
                                            OP.mult, OP.mult))
            block.vector(_)
            m_chain = sv.n

            def _(s, asrc=asrc, dst=dst, war=war, m_chain=m_chain):
                s.wait_ge(sv.h, m_chain)
                wait_war(s, war)
                sa.inc(s.activation(dst, asrc, AF.Identity,
                                    scale=st2[:, 16:17], bias=st2[:, 18:19]))
            block.scalar(_)
            return sa.n

        with ExitStack() as ascope:
            ea = ascope.enter_context
            kfm = ea(nc.sbuf_tensor("kfm", [128, 8 * TOK], BF16))
            ktm_buf = ea(nc.sbuf_tensor("ktm", [128, 1024], BF16))
            ktm_buf2 = ea(nc.sbuf_tensor("ktm2", [128, 1024], BF16))
            vtm = ea(nc.sbuf_tensor("vtm", [128, NT * D], BF16))
            snap = ea(nc.sbuf_tensor("snap", [128, NT * 512], BF16))
            srun = ea(nc.sbuf_tensor("srun", [128, 512], F32))
            sbase = ea(nc.sbuf_tensor("sbase", [128, 512], F32))
            maskT = ea(nc.sbuf_tensor("maskT_s", [128, 128], BF16))
            xlnf = ea(nc.sbuf_tensor("xlnf", [128, 8 * 512], BF16))
            qfm = ea(nc.sbuf_tensor("qfm", [128, 8 * 512], BF16))
            sst = ea(nc.sbuf_tensor("sst", [128, 1024], BF16))
            sattn = ea(nc.sbuf_tensor("sattn", [128, 1024], BF16))
            ybuf = ea(nc.sbuf_tensor("ybuf", [128, 2048], BF16))
            wbig = ea(nc.sbuf_tensor("wbig", [128, 8 * D], BF16))
            wdb = ea(nc.sbuf_tensor("wdb", [128, 2 * 1024], BF16))

            wv_mark = [0]
            wo_mark = [0]
            wchunk_marks = {}

            with nc.Block() as block:
                def _(sy):
                    sd_x.inc(sy.dma_start(out=ident[:], in_=ident_d), 16)
                    sd_x.inc(sy.dma_start(out=maskT[:], in_=mask_d), 16)
                    sd_x.inc(sy.dma_start(out=onesr[:], in_=ones_d), 16)
                    sd_x.inc(sy.dma_start(out=bqc[:], in_=bq_d), 16)
                    sd_x.inc(sy.dma_start(out=bkc[:], in_=bk_d), 16)
                    sd_x.inc(sy.dma_start(out=selb[:], in_=sel_d), 16)
                    sd_x.inc(sy.dma_start(out=smag[:], in_=magic_d), 16)
                    xq_sems = (sd_x, sd_x1, sd_x2, sd_x3)
                    for xq in range(4):
                        xq_sems[xq].inc(sy.dma_start(
                            out=x_res[:, xq * QT * D:(xq + 1) * QT * D]
                                .rearrange("p (t f) -> p t f", t=QT),
                            in_=x_d[xq * QT * 128:(xq + 1) * QT * 128, :]
                                .rearrange("(t p) f -> p t f", p=128)), 16)
                block.sync(_)
                x_mark = sd_x.n

                def _(gp):
                    wv_mark[0] = s_wv.inc(gp.dma_start(
                        out=wbig[:].rearrange("p (dc e) -> p dc e", dc=8),
                        in_=wvT_d.rearrange("(dc p) e -> p dc e", p=128)), 16)
                    s_row.inc(gp.dma_start(out=bvr[:], in_=bv_d), 16)
                    s_row.inc(gp.dma_start(out=bor[:], in_=bo_d), 16)
                    s_row.inc(gp.dma_start(out=gbr[:], in_=gb_d), 16)
                    s_row.inc(gp.dma_start(
                        out=wgate_b[:].rearrange("p (dc e) -> p dc e", dc=8),
                        in_=wgate_d.rearrange("(dc p) e -> p dc e", p=128)), 16)
                block.gpsimd(_)
                row_mark = s_row.n

                def _(v):
                    sv.inc(v.memset(srun[:], 0.0))
                block.vector(_)

            def emit_wchunk(gp, wsrc, buf_idx, key):
                ec = key[1]
                sck = s_ck0 if buf_idx == 0 else s_ck1
                wchunk_marks[key] = (sck, sck.inc(gp.dma_start(
                    out=wdb[:, buf_idx * 1024:(buf_idx + 1) * 1024],
                    in_=wsrc[ec * 128:(ec + 1) * 128, :]), 16))

            def emit_transpose(block, ptr, ptr_free, slot, dstview, apply_mark):
                def _(te, slot=slot, apply_mark=apply_mark, pf=ptr_free[0]):
                    te.wait_ge(sv.h, apply_mark)
                    te.wait_ge(sv.h, pf)
                    for cc in range(8):
                        mm = te.transpose(ptr[:, cc * 128:(cc + 1) * 128],
                                          sapply[:, slot * D + cc * 128:
                                                 slot * D + (cc + 1) * 128],
                                          ident[:])
                    st.inc(mm)
                block.tensor(_)
                m_tr = st.n

                def _(v, m_tr=m_tr, dstview=dstview):
                    v.wait_ge(st.h, m_tr)
                    sv.inc(v.tensor_copy(dstview,
                                         ptr[:].rearrange("p (c w) -> p c w", c=8)))
                block.vector(_)
                ptr_free[0] = sv.n
                return m_tr, sv.n

            if stage >= 1:
              with ExitStack() as ps:
                ptr = ps.enter_context(nc.psum_tensor("ptrA1", [128, 1024], BF16))
                pk0 = ps.enter_context(nc.psum_tensor("pk0", [128, 512], F32))
                pk1 = ps.enter_context(nc.psum_tensor("pk1", [128, 512], F32))
                pv0 = ps.enter_context(nc.psum_tensor("pv0", [128, 1024], F32))
                pv1 = ps.enter_context(nc.psum_tensor("pv1", [128, 1024], F32))
                pds = ps.enter_context(nc.psum_tensor("pds", [128, 512], F32))
                pks = [pk0, pk1]
                pvs = [pv0, pv1]

                with nc.Block() as block:
                    def _(gp):
                        emit_wchunk(gp, wkp_d, 0, ("k", 0, 0))
                        emit_wchunk(gp, wkp_d, 1, ("k", 1, 0))
                    block.gpsimd(_)

                    ptr_free = [sv.n]
                    pv_free = {0: sv.n, 1: sv.n}
                    pds_free = [sv.n]
                    kevac = {}
                    k_evac_all = []
                    vtm_marks = {}
                    ktm_evac = [0]
                    snap_cast = [0]
                    pend_ds = []
                    ktm_bufs = {0: ktm_buf[:, 0:1024], 1: ktm_buf2[:, 0:1024]}

                    def a_srcs(q):
                        return [(x_res[:, t * D:t * D + 512],
                                 x_res[:, t * D + 512:(t + 1) * D])
                                for t in range(q * QT, (q + 1) * QT)]

                    chain_marks = {}
                    def _(v):
                        v.wait_ge(sd_x.h, x_mark)
                    block.vector(_)
                    chain_marks[0] = emit_ln_chain(block, a_srcs(0), 0)

                    for q in range(4):
                        for trel in range(QT):
                            t = q * QT + trel
                            slot = trel % 2
                            m_ap = emit_apply(block, q * QT + trel,
                                              x_res[:, t * D:(t + 1) * D],
                                              sapply[:, slot * D:(slot + 1) * D],
                                              sap_war[slot], chain_marks[q])
                            dstv = xlnf[:].rearrange("p (c w) -> p c w", c=8)[
                                :, :, trel * 128:(trel + 1) * 128]
                            m_tr, _mv = emit_transpose(block, ptr, ptr_free, slot, dstv,
                                                       m_ap)
                            sap_war[slot] = ("t", m_tr)
                        xlnf_mark = sv.n
                        if q + 1 < 4:
                            def _(v, q=q):
                                v.wait_ge((sd_x1, sd_x2, sd_x3)[q].h, 16)
                            block.vector(_)
                            chain_marks[q + 1] = emit_ln_chain(block, a_srcs(q + 1),
                                                               (q + 1) * QT)

                        for ec in range(8):
                            seq = q * 8 + ec
                            def _(te, ec=ec, q=q, seq=seq, xlnf_mark=xlnf_mark):
                                ck_s, ck_m = wchunk_marks[("k", ec, q)]
                                te.wait_ge(ck_s.h, ck_m)
                                te.wait_ge(sv.h, xlnf_mark)
                                if len(k_evac_all) >= 2:
                                    te.wait_ge(sv.h, k_evac_all[-2])
                                for cc in range(8):
                                    mm = te.matmul(pks[seq % 2][:, 0:512],
                                                   wdb[:, (seq % 2) * 1024 + cc * 128:
                                                       (seq % 2) * 1024 + (cc + 1) * 128],
                                                   xlnf[:, cc * 512:(cc + 1) * 512],
                                                   start=(cc == 0), stop=(cc == 7))
                                st.inc(mm)
                            block.tensor(_)
                            m_k = st.n

                            if seq + 2 < 32:
                                def _(gp, seq=seq, m_k=m_k):
                                    gp.wait_ge(st.h, m_k)
                                    q2, ec2 = divmod(seq + 2, 8)
                                    emit_wchunk(gp, wkp_d, (seq + 2) % 2, ("k", ec2, q2))
                                block.gpsimd(_)

                            def _(v, ec=ec, q=q, seq=seq, m_k=m_k):
                                v.wait_ge(st.h, m_k)
                                sv.inc(v.tensor_scalar(
                                    kfm[:, ec * TOK + q * 512:ec * TOK + (q + 1) * 512],
                                    pks[seq % 2][:, 0:512],
                                    bkc[:, ec:ec + 1], None, OP.add))
                            block.vector(_)
                            k_evac_all.append(sv.n)
                        kevac[q] = sv.n

                        for trel in range(QT):
                            t = q * QT + trel
                            pb = t % 2
                            def _(te, trel=trel, t=t, pb=pb, pvf=pv_free[pb]):
                                te.wait_ge(s_wv.h, wv_mark[0])
                                te.wait_ge(s_row.h, row_mark)
                                te.wait_ge(sv.h, pvf)
                                for j in range(2):
                                    for cc in range(8):
                                        te.matmul(pvs[pb][:, j * 512:(j + 1) * 512],
                                                  xlnf[:, cc * 512 + trel * 128:
                                                       cc * 512 + (trel + 1) * 128],
                                                  wbig[:, cc * D + j * 512:cc * D + (j + 1) * 512],
                                                  start=(cc == 0), stop=False)
                                    mm = te.matmul(pvs[pb][:, j * 512:(j + 1) * 512],
                                                   onesr[:],
                                                   bvr[0:1, j * 512:(j + 1) * 512],
                                                   start=False, stop=True)
                                st.inc(mm)
                            block.tensor(_)
                            m_vmm = st.n

                            def _(v, t=t, pb=pb, m_vmm=m_vmm):
                                v.wait_ge(st.h, m_vmm)
                                sv.inc(v.tensor_copy(vtm[:, t * D:(t + 1) * D],
                                                     pvs[pb][:, 0:1024]))
                            block.vector(_)
                            pv_free[pb] = sv.n
                            vtm_marks[t] = sv.n

                            def _(te, t=t, q=q, pf=ptr_free[0]):
                                te.wait_ge(sv.h, kevac[q])
                                te.wait_ge(sv.h, pf)
                                for ec in range(8):
                                    mm = te.transpose(
                                        ptr[:, ec * 128:(ec + 1) * 128],
                                        kfm[:, ec * TOK + t * 128:ec * TOK + (t + 1) * 128],
                                        ident[:])
                                st.inc(mm)
                            block.tensor(_)
                            m_ktr = st.n

                            def _(v, t=t, m_ktr=m_ktr):
                                v.wait_ge(st.h, m_ktr)
                                sv.inc(v.tensor_copy(ktm_bufs[t % 2], ptr[:, 0:1024]))
                            block.vector(_)
                            ptr_free[0] = sv.n
                            ktm_evac[0] = sv.n

                            def emit_ds(tds, ke, kb):
                                def _(te, tds=tds, ke=ke, kb=kb, pdf=pds_free[0]):
                                    te.wait_ge(sv.h, ke)
                                    te.wait_ge(sv.h, vtm_marks[tds])
                                    te.wait_ge(sv.h, pdf)
                                    for h in range(NH):
                                        rb = (h % 2) * 64
                                        cb = (h // 2) * 64
                                        mm = te.matmul(
                                            pds[rb:rb + 64, cb:cb + 64],
                                            kb[:, (h // 2) * 128 + (h % 2) * 64:
                                               (h // 2) * 128 + (h % 2) * 64 + 64],
                                            vtm[:, tds * D + h * 64:tds * D + (h + 1) * 64],
                                            start=True, stop=True)
                                    st.inc(mm)
                                block.tensor(_)
                                m_ds = st.n

                                def _(v, tds=tds, m_ds=m_ds):
                                    v.wait_ge(st.h, m_ds)
                                    v.wait_ge(sv.h, sv.n)
                                    sv.inc(v.tensor_tensor(srun[:], pds[0:128, 0:512],
                                                           srun[:], OP.add))
                                    v.wait_ge(sv.h, sv.n)
                                    sv.inc(v.tensor_copy(snap[:, tds * 512:(tds + 1) * 512],
                                                         srun[:]))
                                block.vector(_)
                                pds_free[0] = sv.n

                            pend_ds.append((t, ktm_evac[0], ktm_bufs[t % 2]))
                            if len(pend_ds) > 1:
                                emit_ds(*pend_ds.pop(0))

                    while pend_ds:
                        emit_ds(*pend_ds.pop(0))

            def emit_cc_gp(block):
                a1_a = sa.n
                a1_v = sv.n

                def _(gp):
                    gp.wait_ge(sa.h, a1_a)
                    gp.wait_ge(sv.h, a1_v)
                    m = sd_cc.inc(gp.dma_start(out=cc_in.ap(), in_=srun[:]), 16)
                    if stage >= 3:
                        emit_wchunk(gp, wqp_d, 0, ("q", 0, 0))
                        emit_wchunk(gp, wqp_d, 1, ("q", 1, 0))
                    gp.wait_ge(sd_cc.h, m)
                    sg.inc(gp.collective_compute(
                        "AllGather", OP.bypass,
                        replica_groups=[[0, 1], [2, 3], [4, 5], [6, 7]],
                        ins=[cc_in.ap().opt()],
                        outs=[cc_out.ap().opt()]))
                    gp.wait_ge(sg.h, sg.n)
                    wo_mark[0] = s_wv.inc(gp.dma_start(
                        out=wbig[:].rearrange("p (dc e) -> p dc e", dc=8),
                        in_=woT_d.rearrange("(dc p) e -> p dc e", p=128)), 16)
                    m2 = sd_cc.inc(gp.dma_start(out=sbase[:], in_=cc_out.ap()[0:128, :]), 16)
                    gp.wait_ge(sd_cc.h, m2)
                block.gpsimd(_)

                def _(v):
                    v.wait_ge(sd_cc.h, sd_cc.n)
                    v.wait_ge(sd_x.h, x_mark)
                    sv.inc(v.tensor_scalar(sbase[:], sbase[:], selb[:, 0:1], None, OP.mult))
                    v.wait_ge(sv.h, sv.n)
                block.vector(_)
                return sv.n

            def emit_fold(block, pool_mark):
                def _(v, pool_mark=pool_mark):
                    v.wait_ge(sv.h, pool_mark)
                    for t in range(NT - 1, 0, -1):
                        sv.inc(v.tensor_tensor(snap[:, t * 512:(t + 1) * 512],
                                               sbase[:],
                                               snap[:, (t - 1) * 512:t * 512], OP.add))
                        v.wait_ge(sv.h, sv.n)
                    sv.inc(v.tensor_copy(snap[:, 0:512], sbase[:]))
                block.vector(_)
                return sv.n

            fold_mark_g = [0]
            if stage == 2:
                with nc.Block() as block:
                    pm = emit_cc_gp(block)
                    fold_mark_g[0] = emit_fold(block, pm)
                    if debug:
                        def _(sy):
                            sy.wait_ge(sv.h, sv.n)
                            sd_o.inc(sy.dma_start(out=dbg["srun"], in_=srun[:]), 16)
                            sd_o.inc(sy.dma_start(out=dbg["sbase"], in_=sbase[:]), 16)
                        block.sync(_)

            if stage >= 3:
              with ExitStack() as ps:
                ptr = ps.enter_context(nc.psum_tensor("ptrA2", [128, 1024], BF16))
                pq0 = ps.enter_context(nc.psum_tensor("pq0", [128, 512], F32))
                pq1 = ps.enter_context(nc.psum_tensor("pq1", [128, 512], F32))
                psc = ps.enter_context(nc.psum_tensor("psc", [128, 1024], F32))
                pattn = ps.enter_context(nc.psum_tensor("pattn", [128, 512], F32))
                py = ps.enter_context(nc.psum_tensor("py", [128, 1024], F32))
                pqs = [pq0, pq1]

                with nc.Block() as block:
                    fold_mark = [fold_mark_g[0]]

                    a1_a = sa.n
                    a1_v = sv.n
                    cc_m = [0]

                    def _(sy):
                        sy.wait_ge(sa.h, a1_a)
                        sy.wait_ge(sv.h, a1_v)
                        cc_m[0] = sd_cc.inc(sy.dma_start(out=cc_in.ap(),
                                                         in_=srun[:]), 16)
                    block.sync(_)

                    def _(gp):
                        gp.wait_ge(sd_cc.h, cc_m[0])
                        sg.inc(gp.collective_compute(
                            "AllGather", OP.bypass,
                            replica_groups=[[0, 1], [2, 3], [4, 5], [6, 7]],
                            ins=[cc_in.ap().opt()],
                            outs=[cc_out.ap().opt()]))
                        emit_wchunk(gp, wqp_d, 0, ("q", 0, 0))
                        emit_wchunk(gp, wqp_d, 1, ("q", 1, 0))
                        wo_mark[0] = s_wv.inc(gp.dma_start(
                            out=wbig[:].rearrange("p (dc e) -> p dc e", dc=8),
                            in_=woT_d.rearrange("(dc p) e -> p dc e", p=128)), 16)
                    block.gpsimd(_)

                    def _(sy):
                        sy.wait_ge(sg.h, sg.n)
                        sd_cc.inc(sy.dma_start(out=sbase[:],
                                               in_=cc_out.ap()[0:128, :]), 16)
                    block.sync(_)

                    ptr_free = [sv.n]
                    py_free = [("v", 0)]
                    ybuf_war = [0]
                    ln2war = {0: 0, 1: 0}
                    q_evac_all = []
                    psc_free = [sv.n]
                    pattn_free = [sv.n]

                    def a_srcs(q):
                        return [(x_res[:, t * D:t * D + 512],
                                 x_res[:, t * D + 512:(t + 1) * D])
                                for t in range(q * QT, (q + 1) * QT)]

                    chain_marks = {}
                    chain_marks[0] = emit_ln_chain(block, a_srcs(0), 0)
                    pend_wo = [None]

                    def emit_wo_stage(t, trel, am):
                        def _(te, m=am, pyf=py_free[0]):
                            te.wait_ge(sv.h, m)
                            te.wait_ge(s_wv.h, wo_mark[0])
                            te.wait_ge(s_row.h, row_mark)
                            wait_war(te, pyf)
                            for j in range(2):
                                for cc in range(8):
                                    te.matmul(py[:, j * 512:(j + 1) * 512],
                                              sattn[:, cc * 128:(cc + 1) * 128],
                                              wbig[:, cc * D + j * 512:cc * D + (j + 1) * 512],
                                              start=(cc == 0), stop=False)
                                mm = te.matmul(py[:, j * 512:(j + 1) * 512],
                                               onesr[:],
                                               bor[0:1, j * 512:(j + 1) * 512],
                                               start=False, stop=True)
                            st.inc(mm)
                        block.tensor(_)
                        m_wo = st.n

                        if a2sub < 6:
                            return
                        sy_ = trel % 2

                        def _(v, m_wo=m_wo, sy_=sy_):
                            v.wait_ge(st.h, m_wo)
                            sv.inc(v.tensor_copy(
                                ybuf[:, sy_ * 1024:(sy_ + 1) * 1024], py[:, 0:1024]))
                        block.vector(_)
                        py_free[0] = ("v", sv.n)

                        if trel % 2 == 1:
                            par = (trel // 2) % 2
                            base2 = par * 2
                            chP = emit_ln_chain(
                                block,
                                [(ybuf[:, 0:512], ybuf[:, 512:1024]),
                                 (ybuf[:, 1024:1536], ybuf[:, 1536:2048])],
                                base2)
                            aps = []
                            res_m = 0
                            for j in range(2):
                                tt = t - 1 + j
                                m_apj = emit_apply(block, base2 + j,
                                                   ybuf[:, j * 1024:(j + 1) * 1024],
                                                   ybuf[:, j * 1024:(j + 1) * 1024],
                                                   ("t", 0), chP)
                                aps.append(m_apj)

                                def _(v, tt=tt, j=j):
                                    v.wait_ge(sv.h, sv.n)
                                    sv.inc(v.tensor_tensor(
                                        x_res[:, tt * D:(tt + 1) * D],
                                        x_res[:, tt * D:(tt + 1) * D],
                                        ybuf[:, j * 1024:(j + 1) * 1024], OP.add))
                                block.vector(_)
                                res_m = sv.n
                            ln2war[par] = aps[-1]
                            ybuf_war[0] = aps[-1]
                            ybuf_res = res_m

                    for q in range(4):
                        for trel in range(QT):
                            t = q * QT + trel
                            slot = trel % 2
                            m_ap = emit_apply(block, q * QT + trel,
                                              x_res[:, t * D:(t + 1) * D],
                                              sapply[:, slot * D:(slot + 1) * D],
                                              sap_war[slot], chain_marks[q])
                            dstv = xlnf[:].rearrange("p (c w) -> p c w", c=8)[
                                :, :, trel * 128:(trel + 1) * 128]
                            m_tr, _mv = emit_transpose(block, ptr, ptr_free, slot, dstv,
                                                       m_ap)
                            sap_war[slot] = ("t", m_tr)
                        xlnf_mark = sv.n
                        if q + 1 < 4:
                            chain_marks[q + 1] = emit_ln_chain(block, a_srcs(q + 1),
                                                               (q + 1) * QT)

                        for ec in range(8 if a2sub >= 2 else 0):
                            seq = q * 8 + ec
                            def _(te, ec=ec, q=q, seq=seq, xlnf_mark=xlnf_mark):
                                ck_s, ck_m = wchunk_marks[("q", ec, q)]
                                te.wait_ge(ck_s.h, ck_m)
                                te.wait_ge(sv.h, xlnf_mark)
                                if len(q_evac_all) >= 2:
                                    te.wait_ge(sv.h, q_evac_all[-2])
                                for cc in range(8):
                                    mm = te.matmul(pqs[seq % 2][:, 0:512],
                                                   wdb[:, (seq % 2) * 1024 + cc * 128:
                                                       (seq % 2) * 1024 + (cc + 1) * 128],
                                                   xlnf[:, cc * 512:(cc + 1) * 512],
                                                   start=(cc == 0), stop=(cc == 7))
                                st.inc(mm)
                            block.tensor(_)
                            m_q = st.n

                            if seq + 2 < 32:
                                def _(gp, seq=seq, m_q=m_q):
                                    gp.wait_ge(st.h, m_q)
                                    q2, ec2 = divmod(seq + 2, 8)
                                    emit_wchunk(gp, wqp_d, (seq + 2) % 2, ("q", ec2, q2))
                                block.gpsimd(_)

                            def _(v, ec=ec, seq=seq, m_q=m_q):
                                v.wait_ge(st.h, m_q)
                                sv.inc(v.tensor_scalar(qfm[:, ec * 512:(ec + 1) * 512],
                                                       pqs[seq % 2][:, 0:512],
                                                       bqc[:, ec:ec + 1], None, OP.add))
                            block.vector(_)
                            q_evac_all.append(sv.n)
                        qdone = sv.n

                        if q == 0:
                            def _(v):
                                v.wait_ge(sd_cc.h, sd_cc.n)
                                v.wait_ge(sd_x.h, x_mark)
                                v.wait_ge(sv.h, sv.n)
                                sv.inc(v.tensor_scalar(sbase[:], sbase[:],
                                                       selb[:, 0:1], None, OP.mult))
                                v.wait_ge(sv.h, sv.n)
                            block.vector(_)
                            fold_mark[0] = emit_fold(block, sv.n)
                            if debug:
                                def _(sy):
                                    sy.wait_ge(sv.h, sv.n)
                                    sd_o.inc(sy.dma_start(out=dbg["srun"],
                                                          in_=srun[:]), 16)
                                    sd_o.inc(sy.dma_start(out=dbg["sbase"],
                                                          in_=sbase[:]), 16)
                                block.sync(_)

                        if pend_wo[0] is not None:
                            emit_wo_stage(*pend_wo[0])
                            pend_wo[0] = None

                        for trel in range(QT if a2sub >= 3 else 0):
                            t = q * QT + trel
                            attn_marks = []
                            sstb = [sst, ktm_buf]
                            sst_marks = {}
                            for hg in range(2):
                                def _(te, t=t, trel=trel, hg=hg, qdone=qdone,
                                      pf=psc_free[0]):
                                    te.wait_ge(sv.h, qdone)
                                    te.wait_ge(sv.h, pf)
                                    for hrel in range(8):
                                        h = hg * 8 + hrel
                                        rb = (h % 2) * 64
                                        ec = h // 2
                                        pos = (hrel % 2) * 512 + (hrel // 2) * 128
                                        mm = te.matmul(
                                            psc[:, pos:pos + 128],
                                            kfm[rb:rb + 64, ec * TOK + t * 128:
                                                ec * TOK + (t + 1) * 128],
                                            qfm[rb:rb + 64, ec * 512 + trel * 128:
                                                ec * 512 + (trel + 1) * 128],
                                            start=True, stop=True)
                                    st.inc(mm)
                                block.tensor(_)
                                m_sc = st.n

                                def _(v, hg=hg, m_sc=m_sc):
                                    v.wait_ge(st.h, m_sc)
                                    for hrel in range(8):
                                        pos = (hrel % 2) * 512 + (hrel // 2) * 128
                                        mm = v.tensor_tensor(
                                            sstb[hg][:, pos:pos + 128],
                                            psc[:, pos:pos + 128],
                                            maskT[:], OP.mult)
                                    sv.inc(mm)
                                block.vector(_)
                                sst_marks[hg] = sv.n
                                psc_free[0] = sv.n

                                if hg == 0 and pend_wo[0] is not None:
                                    emit_wo_stage(*pend_wo[0])
                                    pend_wo[0] = None

                            if a2sub < 4:
                                continue

                            for hg in range(2):
                                def _(te, t=t, trel=trel, hg=hg, m_sstv=sst_marks[hg],
                                      paf=pattn_free[0], fm=fold_mark[0]):
                                    te.wait_ge(sv.h, m_sstv)
                                    te.wait_ge(sv.h, paf)
                                    te.wait_ge(sv.h, fm)
                                    for hrel in range(8):
                                        h = hg * 8 + hrel
                                        rb = (h % 2) * 64
                                        ec = h // 2
                                        ecrel = ec - hg * 4
                                        pos = (hrel % 2) * 512 + (hrel // 2) * 128
                                        te.matmul(
                                            pattn[rb:rb + 64, ecrel * 128:(ecrel + 1) * 128],
                                            vtm[:, t * D + h * 64:t * D + (h + 1) * 64],
                                            sstb[hg][:, pos:pos + 128],
                                            start=True, stop=False)
                                        mm = te.matmul(
                                            pattn[rb:rb + 64, ecrel * 128:(ecrel + 1) * 128],
                                            snap[rb:rb + 64, t * 512 + ec * 64:
                                                 t * 512 + ec * 64 + 64],
                                            qfm[rb:rb + 64, ec * 512 + trel * 128:
                                                ec * 512 + (trel + 1) * 128],
                                            start=False, stop=True)
                                    st.inc(mm)
                                block.tensor(_)
                                m_at = st.n

                                def _(v, hg=hg, m_at=m_at):
                                    v.wait_ge(st.h, m_at)
                                    sv.inc(v.tensor_copy(sattn[:, hg * 512:(hg + 1) * 512],
                                                         pattn[:, 0:512]))
                                block.vector(_)
                                attn_marks.append(sv.n)
                                pattn_free[0] = sv.n

                            if a2sub < 5:
                                continue
                            pend_wo[0] = (t, trel, attn_marks[-1])

                    if pend_wo[0] is not None:
                        emit_wo_stage(*pend_wo[0])
                        pend_wo[0] = None

            if debug and stage >= 3:
                with nc.Block() as block:
                    def _(sy):
                        sy.wait_ge(sv.h, sv.n)
                        sd_o.inc(sy.dma_start(
                            out=dbg["xattn"].rearrange("(t p) f -> p t f", p=128),
                            in_=x_res[:].rearrange("p (t f) -> p t f", t=NT)), 16)
                    block.sync(_)

        if stage >= 4:
          with ExitStack() as fscope:
            ef = fscope.enter_context
            xlnfF = ef(nc.sbuf_tensor("xlnfF", [128, 8 * 1024], BF16))
            gwork = ef(nc.sbuf_tensor("gwork", [128, 384], F32))
            hid = ef(nc.sbuf_tensor("hid", [128, 8 * 1024], BF16))
            hsh = ef(nc.sbuf_tensor("hsh", [128, 8 * 1024], BF16))
            accum = ef(nc.sbuf_tensor("accum", [128, 8 * 1024], F32))
            wdense = ef(nc.sbuf_tensor("wdense", [128, 128], F32))
            wpair = ef(nc.sbuf_tensor("wpair", [128, 2 * 2048], BF16))
            wout_s = ef(nc.sbuf_tensor("wout_s", [128, 8 * 1024], BF16))
            w2A = ef(nc.sbuf_tensor("w2A", [128, 8 * 1024], BF16))
            ssil = ef(nc.sbuf_tensor("ssil", [128, 1024], F32))

            fmarks = {}
            hid_mark = 0
            f1_state = {}

            def make_f1ab_steps(hf, base0):
                T0 = hf * 8
                WD = hf * 64
                steps = []

                def f_srcs(hg):
                    return [(x_res[:, t * D:t * D + 512],
                             x_res[:, t * D + 512:(t + 1) * D])
                            for t in range(T0 + hg * 4, T0 + (hg + 1) * 4)]

                for half_grp in range(2):
                    def chain_step(block, ptr, pg, hg=half_grp):
                        f1_state[(hf, "chain", hg)] = emit_ln_chain(
                            block, f_srcs(hg), base0 + hg * 4)
                    steps.append(chain_step)

                    for k in range(QT):
                        def apply_step(block, ptr, pg, hg=half_grp, k=k):
                            trel = hg * 4 + k
                            t = T0 + trel
                            slot = trel % 2
                            m_ap = emit_apply(block, base0 + trel,
                                              x_res[:, t * D:(t + 1) * D],
                                              sapply[:, slot * D:(slot + 1) * D],
                                              sap_war[slot], f1_state[(hf, "chain", hg)])
                            f1_state[(hf, "ap", trel)] = m_ap
                            f1_state[(hf, "apslot", trel)] = slot
                        steps.append(apply_step)

                        def tr_step(block, ptr, pg, hg=half_grp, k=k):
                            trel = hg * 4 + k
                            slot = f1_state[(hf, "apslot", trel)]
                            pf = f1_state.setdefault((hf, "ptr_free"), [sv.n])
                            dstv = xlnfF[:].rearrange("p (c w) -> p c w", c=8)[
                                :, :, trel * 128:(trel + 1) * 128]
                            m_tr, m_xl = emit_transpose(block, ptr, pf, slot, dstv,
                                                        f1_state[(hf, "ap", trel)])
                            sap_war[slot] = ("t", m_tr)
                            f1_state[(hf, "xl", trel)] = m_xl
                        steps.append(tr_step)

                        def gate_step(block, ptr, pg, hg=half_grp, k=k):
                            trel = hg * 4 + k
                            pgf = f1_state.setdefault((hf, "pg_free"), [sv.n])

                            def _(te, trel=trel, m_xl=f1_state[(hf, "xl", trel)],
                                  pgfv=pgf[0]):
                                te.wait_ge(sv.h, m_xl)
                                te.wait_ge(sv.h, pgfv)
                                te.wait_ge(s_row.h, row_mark)
                                for cc in range(8):
                                    te.matmul(pg[:, 0:8],
                                              xlnfF[:, cc * 1024 + trel * 128:
                                                   cc * 1024 + (trel + 1) * 128],
                                              wgate_b[:, cc * 8:(cc + 1) * 8],
                                              start=(cc == 0), stop=False)
                                mm = te.matmul(pg[:, 0:8], onesr[:], gbr[0:1, 0:8],
                                               start=False, stop=True)
                                st.inc(mm)
                            block.tensor(_)
                            m_g = st.n

                            def _(v, trel=trel, m_g=m_g):
                                v.wait_ge(st.h, m_g)
                                v.wait_ge(sv.h, sv.n)
                                sv.inc(v.tensor_copy(
                                    gwork[:, LG + trel * 8:LG + (trel + 1) * 8],
                                    pg[:, 0:8]))
                            block.vector(_)
                            pgf[0] = sv.n
                        steps.append(gate_step)

                def softmax_step(block, ptr, pg):
                    def _(v):
                        def step(inst):
                            sv.inc(inst)
                            v.wait_ge(sv.h, sv.n)
                        v.wait_ge(sv.h, sv.n)
                        mm = None
                        for ti in range(8):
                            mm = v.reduce_max(gwork[:, GMX + ti:GMX + ti + 1],
                                              gwork[:, LG + ti * 8:LG + (ti + 1) * 8],
                                              axis=AX.X)
                        step(mm)
                        for ti in range(8):
                            mm = v.tensor_scalar(gwork[:, EB + ti * 8:EB + (ti + 1) * 8],
                                                 gwork[:, LG + ti * 8:LG + (ti + 1) * 8],
                                                 gwork[:, GMX + ti:GMX + ti + 1], None,
                                                 OP.subtract)
                        step(mm)
                    block.vector(_)
                    m_pre = sv.n

                    def _(s, m_pre=m_pre):
                        s.wait_ge(sv.h, m_pre)
                        sa.inc(s.activation(gwork[:, PB:PB + 64], gwork[:, EB:EB + 64],
                                            AF.Exp))
                    block.scalar(_)
                    m_exp = sa.n

                    def _(v, m_exp=m_exp):
                        def step(inst):
                            sv.inc(inst)
                            v.wait_ge(sv.h, sv.n)
                        v.wait_ge(sa.h, m_exp)
                        v.wait_ge(sv.h, sv.n)
                        mm = None
                        for ti in range(8):
                            mm = v.reduce_sum(gwork[:, GSUM + ti:GSUM + ti + 1],
                                              gwork[:, PB + ti * 8:PB + (ti + 1) * 8],
                                              axis=AX.X)
                        step(mm)
                        step(v.reciprocal(gwork[:, GRS:GRS + 8], gwork[:, GSUM:GSUM + 8]))
                        for ti in range(8):
                            mm = v.tensor_scalar(gwork[:, PB + ti * 8:PB + (ti + 1) * 8],
                                                 gwork[:, PB + ti * 8:PB + (ti + 1) * 8],
                                                 gwork[:, GRS + ti:GRS + ti + 1], None,
                                                 OP.mult)
                        step(mm)
                        for ti in range(8):
                            mm = v.reduce_max(gwork[:, GM1 + ti:GM1 + ti + 1],
                                              gwork[:, PB + ti * 8:PB + (ti + 1) * 8],
                                              axis=AX.X)
                        step(mm)
                        for ti in range(8):
                            mm = v.tensor_scalar(gwork[:, GE + ti * 8:GE + (ti + 1) * 8],
                                                 gwork[:, PB + ti * 8:PB + (ti + 1) * 8],
                                                 gwork[:, GM1 + ti:GM1 + ti + 1], None,
                                                 OP.is_ge)
                        step(mm)
                        for ti in range(8):
                            mm = v.scalar_tensor_tensor(
                                gwork[:, MB + ti * 8:MB + (ti + 1) * 8],
                                gwork[:, GE + ti * 8:GE + (ti + 1) * 8], NEG,
                                gwork[:, PB + ti * 8:PB + (ti + 1) * 8],
                                OP.mult, OP.add)
                        step(mm)
                        for ti in range(8):
                            mm = v.reduce_max(gwork[:, GM2 + ti:GM2 + ti + 1],
                                              gwork[:, MB + ti * 8:MB + (ti + 1) * 8],
                                              axis=AX.X)
                        step(mm)
                        for ti in range(8):
                            mm = v.tensor_scalar(gwork[:, GE + ti * 8:GE + (ti + 1) * 8],
                                                 gwork[:, PB + ti * 8:PB + (ti + 1) * 8],
                                                 gwork[:, GM2 + ti:GM2 + ti + 1], None,
                                                 OP.is_ge)
                        step(mm)
                        for ti in range(8):
                            mm = v.tensor_tensor(wdense[:, WD + ti * 8:WD + (ti + 1) * 8],
                                                 gwork[:, PB + ti * 8:PB + (ti + 1) * 8],
                                                 gwork[:, GE + ti * 8:GE + (ti + 1) * 8],
                                                 OP.mult)
                        step(mm)
                    block.vector(_)
                steps.append(softmax_step)
                return steps

            f1ab_steps = {0: make_f1ab_steps(0, 0), 1: make_f1ab_steps(1, 8)}

            for hf in range(2):
                if hf == 1 and stage < 6:
                    continue
                T0 = hf * 8
                with ExitStack() as ps:
                    ptr = ps.enter_context(nc.psum_tensor(f"ptrF{hf}", [128, 1024], BF16))
                    pg = ps.enter_context(nc.psum_tensor(f"pg{hf}", [128, 8], F32))
                    pua0 = ps.enter_context(nc.psum_tensor(f"pua0{hf}", [128, 512], F32))
                    pub0 = ps.enter_context(nc.psum_tensor(f"pub0{hf}", [128, 512], F32))
                    pua1 = ps.enter_context(nc.psum_tensor(f"pua1{hf}", [128, 512], F32))
                    pub1 = ps.enter_context(nc.psum_tensor(f"pub1{hf}", [128, 512], F32))
                    puas = [pua0, pua1]
                    pubs = [pub0, pub1]
                    psh = ps.enter_context(nc.psum_tensor(f"psh{hf}", [128, 1024], F32))

                    def emit_pair(gp, wkey, hf, pp, src_d):
                        swp = s_wp0 if pp % 2 == 0 else s_wp1
                        fmarks[(wkey, hf, pp)] = (swp, swp.inc(gp.dma_start(
                            out=wpair[:, (pp % 2) * 2048:(pp % 2) * 2048 + 2048]
                                .rearrange("p (dc e) -> p dc e", dc=8),
                            in_=src_d[:, pp * 256:(pp + 1) * 256]
                                .rearrange("(dc p) e -> p dc e", p=128)), 16))

                    with nc.Block() as block:
                        def _(gp, hf=hf):
                            emit_pair(gp, "w1", hf, 0, w1p_d)
                            emit_pair(gp, "w1", hf, 1, w1p_d)
                        block.gpsimd(_)

                        for stp in f1ab_steps[hf]:
                            stp(block, ptr, pg)

                        pua_free = {0: sa.n, 1: sa.n}
                        pub_free = {0: sv.n, 1: sv.n}
                        ub_ix = [0]
                        for dst, wkey, wsrc_d in ((hid, "w1", w1p_d), (hsh, "win", winp_d)):
                            if wkey == "win":
                                def _(gp, hf=hf, m=fmarks[("w1mm", hf)]):
                                    gp.wait_ge(st.h, m)
                                    emit_pair(gp, "win", hf, 0, winp_d)
                                    emit_pair(gp, "win", hf, 1, winp_d)
                                    fmarks[("wout", hf)] = (s_wB, s_wB.inc(gp.dma_start(
                                        out=wout_s[:].rearrange("p (dc e) -> p dc e", dc=8),
                                        in_=wout_d.rearrange("(dc p) e -> p dc e", p=128)),
                                        16))
                                    fmarks[("w2", hf, 0)] = (s_wA, s_wA.inc(gp.dma_start(
                                        out=w2A[:].rearrange("p (dc e) -> p dc e", dc=8),
                                        in_=w2_d[0 * D:(0 + 1) * D, :]
                                            .rearrange("(dc p) e -> p dc e", p=128)), 16))
                                block.gpsimd(_)

                            for pp in range(8):
                                wslice = wpair[:, (pp % 2) * 2048:(pp % 2) * 2048 + 2048]
                                for tg in range(2):
                                    ub = ub_ix[0]
                                    ub_ix[0] = 1 - ub
                                    def _(te, pp=pp, tg=tg, wkey=wkey, hf=hf, wslice=wslice,
                                          ub=ub, puaf=pua_free[ub], pubf=pub_free[ub]):
                                        wp_s, wp_m = fmarks[(wkey, hf, pp)]
                                        te.wait_ge(wp_s.h, wp_m)
                                        te.wait_ge(sa.h, puaf)
                                        te.wait_ge(sv.h, pubf)
                                        for cc in range(8):
                                            mma = te.matmul(
                                                puas[ub][:, 0:512],
                                                wslice[:, cc * 256:cc * 256 + 128],
                                                xlnfF[:, cc * 1024 + tg * 512:
                                                      cc * 1024 + (tg + 1) * 512],
                                                start=(cc == 0), stop=(cc == 7))
                                        st.inc(mma)
                                        for cc in range(8):
                                            mmb = te.matmul(
                                                pubs[ub][:, 0:512],
                                                wslice[:, cc * 256 + 128:cc * 256 + 256],
                                                xlnfF[:, cc * 1024 + tg * 512:
                                                      cc * 1024 + (tg + 1) * 512],
                                                start=(cc == 0), stop=(cc == 7))
                                        st.inc(mmb)
                                    block.tensor(_)
                                    m_a_mm = st.n - 1
                                    m_b_mm = st.n

                                    if tg == 1 and pp + 2 < 8:
                                        def _(gp, wkey=wkey, hf=hf, pp=pp, m_b_mm=m_b_mm,
                                              src_d=wsrc_d):
                                            gp.wait_ge(st.h, m_b_mm)
                                            emit_pair(gp, wkey, hf, pp + 2, src_d)
                                        block.gpsimd(_)

                                    def _(s, ub=ub, m_a_mm=m_a_mm, pubf=pub_free[ub]):
                                        s.wait_ge(st.h, m_a_mm)
                                        s.wait_ge(sv.h, pubf)
                                        sa.inc(s.activation(ssil[:, ub * 512:(ub + 1) * 512],
                                                            puas[ub][:, 0:512], AF.Silu))
                                    block.scalar(_)
                                    pua_free[ub] = sa.n
                                    m_sil = sa.n

                                    def _(v, dst=dst, pp=pp, tg=tg, ub=ub, m_b_mm=m_b_mm,
                                          m_sil=m_sil):
                                        v.wait_ge(st.h, m_b_mm)
                                        v.wait_ge(sa.h, m_sil)
                                        sv.inc(v.tensor_tensor(
                                            dst[:, pp * 1024 + tg * 512:
                                                pp * 1024 + (tg + 1) * 512],
                                            pubs[ub][:, 0:512], ssil[:, ub * 512:(ub + 1) * 512],
                                            OP.mult))
                                    block.vector(_)
                                    pub_free[ub] = sv.n
                            if wkey == "w1":
                                fmarks[("w1mm", hf)] = st.n
                        hid_mark = sv.n

                        psh_free = [sv.n]
                        for trel in range(8):
                            def _(te, trel=trel, hf=hf, hid_mark=hid_mark, pshf=psh_free[0]):
                                wo_s, wo_m = fmarks[("wout", hf)]
                                te.wait_ge(sv.h, hid_mark)
                                te.wait_ge(wo_s.h, wo_m)
                                te.wait_ge(sv.h, pshf)
                                for j in range(2):
                                    for jp in range(8):
                                        mm = te.matmul(
                                            psh[:, j * 512:(j + 1) * 512],
                                            hsh[:, jp * 1024 + trel * 128:
                                                jp * 1024 + (trel + 1) * 128],
                                            wout_s[:, jp * 1024 + j * 512:
                                                   jp * 1024 + (j + 1) * 512],
                                            start=(jp == 0), stop=(jp == 7))
                                st.inc(mm)
                            block.tensor(_)
                            m_s = st.n

                            def _(v, trel=trel, m_s=m_s):
                                v.wait_ge(st.h, m_s)
                                v.wait_ge(sv.h, sv.n)
                                sv.inc(v.tensor_copy(accum[:, trel * 1024:(trel + 1) * 1024],
                                                     psh[:, 0:1024]))
                            block.vector(_)
                            psh_free[0] = sv.n
                        fmarks[("fs_done", hf)] = st.n

                if stage < 5:
                    continue
                with ExitStack() as ps:
                    pr0 = ps.enter_context(nc.psum_tensor(f"pr0_{hf}", [128, 1024], F32))
                    pr1 = ps.enter_context(nc.psum_tensor(f"pr1_{hf}", [128, 1024], F32))
                    prs = [pr0, pr1]
                    do_f1ab_h1 = False
                    if do_f1ab_h1:
                        ptrF2 = ps.enter_context(nc.psum_tensor("ptrF2", [128, 1024], BF16))
                        pgF2 = ps.enter_context(nc.psum_tensor("pgF2", [128, 8], F32))

                    with nc.Block() as block:
                        n_cons = [0]

                        def consume_h1(upto):
                            while do_f1ab_h1 and n_cons[0] < upto:
                                f1ab_steps[1][n_cons[0]](block, ptrF2, pgF2)
                                n_cons[0] += 1

                        last_mm_of_expert = {}
                        for e in range(8):
                            slot = w2A if (e % 2 == 0) else wout_s
                            if e != 0:
                                def _(gp, e=e, hf=hf, slot=slot):
                                    if e >= 2:
                                        gp.wait_ge(st.h, last_mm_of_expert[e - 2])
                                    sw2 = s_wA if e % 2 == 0 else s_wB
                                    fmarks[("w2", hf, e)] = (sw2, sw2.inc(gp.dma_start(
                                        out=slot[:].rearrange("p (dc e2) -> p dc e2", dc=8),
                                        in_=w2_d[e * D:(e + 1) * D, :]
                                            .rearrange("(dc p) e2 -> p dc e2", p=128)), 16))
                                block.gpsimd(_)

                            pr_free = {0: sv.n, 1: sv.n}
                            for trel in range(8):
                                pb = trel % 2
                                def _(te, e=e, trel=trel, pb=pb, hf=hf, slot=slot,
                                      prf=pr_free[pb], hid_mark=hid_mark):
                                    w2_s, w2_m = fmarks[("w2", hf, e)]
                                    te.wait_ge(w2_s.h, w2_m)
                                    te.wait_ge(sv.h, prf)
                                    te.wait_ge(sv.h, hid_mark)
                                    for j in range(2):
                                        for jp in range(8):
                                            mm = te.matmul(
                                                prs[pb][:, j * 512:(j + 1) * 512],
                                                hid[:, jp * 1024 + trel * 128:
                                                    jp * 1024 + (trel + 1) * 128],
                                                slot[:, jp * 1024 + j * 512:
                                                     jp * 1024 + (j + 1) * 512],
                                                start=(jp == 0), stop=(jp == 7))
                                    st.inc(mm)
                                block.tensor(_)
                                m_r = st.n

                                def _(v, e=e, trel=trel, pb=pb, m_r=m_r):
                                    v.wait_ge(st.h, m_r)
                                    v.wait_ge(sv.h, sv.n)
                                    sv.inc(v.scalar_tensor_tensor(
                                        accum[:, trel * 1024:(trel + 1) * 1024],
                                        prs[pb][:, 0:1024],
                                        wdense[:, hf * 64 + trel * 8 + e:
                                                hf * 64 + trel * 8 + e + 1],
                                        accum[:, trel * 1024:(trel + 1) * 1024],
                                        OP.mult, OP.add))
                                block.vector(_)
                                pr_free[pb] = sv.n
                                if e == 7 and trel == 3:
                                    ln4c = [emit_ln_chain(
                                        block,
                                        [(accum[:, r * 1024:r * 1024 + 512],
                                          accum[:, r * 1024 + 512:(r + 1) * 1024])
                                         for r in range(4)], 0)]
                            last_mm_of_expert[e] = st.n
                            consume_h1((e + 1) * 27 // 8)

                        consume_h1(27)

                        res_marks = []
                        for half_grp in range(2):
                            if half_grp == 0:
                                m_chain = ln4c[0]
                            else:
                                srcs4 = [(accum[:, trel * 1024:trel * 1024 + 512],
                                          accum[:, trel * 1024 + 512:(trel + 1) * 1024])
                                         for trel in range(4, 8)]
                                m_chain = emit_ln_chain(block, srcs4, 4)
                            for k in range(QT):
                                trel = half_grp * 4 + k
                                t = T0 + trel
                                slot = trel % 2
                                m4 = emit_apply(block, trel,
                                                accum[:, trel * 1024:(trel + 1) * 1024],
                                                sapply[:, slot * D:(slot + 1) * D],
                                                sap_war[slot], m_chain)

                                def _(v, t=t, slot=slot):
                                    v.wait_ge(sv.h, sv.n)
                                    sv.inc(v.tensor_tensor(
                                        x_res[:, t * D:(t + 1) * D],
                                        x_res[:, t * D:(t + 1) * D],
                                        sapply[:, slot * D:(slot + 1) * D], OP.add))
                                block.vector(_)
                                sap_war[slot] = ("v", sv.n)
                                res_marks.append(sv.n)

                        def _(sy, hf=hf, m=res_marks[-1]):
                            sy.wait_ge(sv.h, m)
                            sd_o.inc(sy.dma_start(
                                out=y_d.rearrange("(t p) f -> p t f", p=128)[
                                    :, hf * 8:(hf + 1) * 8, :],
                                in_=x_res[:, hf * 8 * D:(hf + 1) * 8 * D]
                                    .rearrange("p (t f) -> p t f", t=8)), 16)
                        block.sync(_)

            if debug:
                with nc.Block() as block:
                    def _(sy):
                        sy.wait_ge(sv.h, sv.n)
                        sd_o.inc(sy.dma_start(out=dbg["wdense"], in_=wdense[:]), 16)
                        sd_o.inc(sy.dma_start(out=dbg["hid"], in_=hid[:]), 16)
                        sd_o.inc(sy.dma_start(out=dbg["hsh"], in_=hsh[:]), 16)
                        sd_o.inc(sy.dma_start(out=dbg["accum"], in_=accum[:]), 16)
                        sy.wait_ge(sd_o.h, sd_o.n)
                    block.sync(_)
            else:
                with nc.Block() as block:
                    def _(sy):
                        sy.wait_ge(sd_o.h, sd_o.n)
                    block.sync(_)
        else:
            with nc.Block() as block:
                def _(sy):
                    sy.wait_ge(sd_o.h, sd_o.n)
                block.sync(_)

    return nc


def make_inputs(inputs):
    x = np.ascontiguousarray(inputs["L0_attn_x"], dtype=np.float32)
    wqT = np.ascontiguousarray(inputs["L0_attn_wq_w"].T)
    wkT = np.ascontiguousarray(inputs["L0_attn_wk_w"].T)
    def packch(w):
        return np.ascontiguousarray(
            w.reshape(8, 128, 8, 128).transpose(2, 1, 0, 3).reshape(1024, 1024))
    wkp = packch(wkT)
    wqp = packch(wqT)
    wvT = np.ascontiguousarray(inputs["L0_attn_wv_w"].T)
    woT = np.ascontiguousarray(inputs["L0_attn_wo_w"].T)
    perm = np.concatenate([np.r_[p * 128:(p + 1) * 128, (p + 8) * 128:(p + 9) * 128]
                           for p in range(8)])
    w1p = np.ascontiguousarray(inputs["L0_ffn_w1_shared"][:, perm])
    winp = np.ascontiguousarray(inputs["L0_ffn_w_shared_in"][:, perm])
    wout = np.ascontiguousarray(inputs["L0_ffn_w_shared_out"])
    wgate = np.ascontiguousarray(inputs["L0_ffn_w_gate"])
    w2 = np.ascontiguousarray(inputs["L0_ffn_w2"].reshape(8 * 1024, 1024))
    bqc = np.ascontiguousarray(inputs["L0_attn_wq_b"].reshape(8, 128).T)
    bkc = np.ascontiguousarray(inputs["L0_attn_wk_b"].reshape(8, 128).T)
    bvr = np.ascontiguousarray(inputs["L0_attn_wv_b"].reshape(1, 1024))
    bor = np.ascontiguousarray(inputs["L0_attn_wo_b"].reshape(1, 1024))
    gbr = np.ascontiguousarray(inputs["L0_ffn_gate_bias"].reshape(1, 8))
    ident = np.eye(128, dtype=BF)
    maskT = np.triu(np.ones((128, 128), np.float32)).astype(BF)
    onesr = np.ones((1, 128), dtype=BF)
    magici = np.full((128, 8), MAGIC, np.int32)

    in_maps = []
    for c in range(8):
        b, hf = c // 2, c % 2
        sel = np.zeros((128, 8), np.float32)
        sel[:, 0] = float(hf)
        in_maps.append(dict(
            x=np.ascontiguousarray(x[b, hf * 2048:(hf + 1) * 2048]),
            wqT=wqT, wkT=wkT, wkp=wkp, wqp=wqp, wvT=wvT, woT=woT,
            w1p=w1p, winp=winp, wout=wout, wgate=wgate, w2=w2,
            bqc=bqc, bkc=bkc, bvr=bvr, bor=bor, gbr=gbr,
            ident=ident, maskT=maskT, onesr=onesr, sel=sel, magici=magici,
        ))
    return in_maps


def kernel(**inputs):
    in_maps = make_inputs(inputs)
    nc = build(debug=False)
    res = run_bass_kernel_spmd(nc, in_maps, core_ids=list(range(8)))
    out = np.empty((4, 4096, 1024), np.float32)
    for c in range(8):
        b, hf = c // 2, c % 2
        out[b, hf * 2048:(hf + 1) * 2048] = res.results[c]["y"]
    return out
